# version 10
# baseline (speedup 1.0000x reference)
import numpy as np
import ml_dtypes
import concourse.bass as bass
import concourse.mybir as mybir
from concourse.bass_utils import run_bass_kernel_spmd
from contextlib import ExitStack

F32 = mybir.dt.float32
BF16 = mybir.dt.bfloat16
AF = mybir.ActivationFunctionType
AL = mybir.AluOpType

TOK = 1024
E = 4096
NH = 32
D = 128
HFF = 11008
NHT = 86
NYF = 22
SCALE = float(1.0 / np.sqrt(128.0))
SHIFT = -12.0
EPS = 1e-5

TAPS = ()
SIM_LOCAL = False


def build_nc():
    nc = bass.Bass("TRN2", target_bir_lowering=False)
    hx = nc.dram_tensor("x", [TOK, E], F32, kind="ExternalInput").ap()
    hwq = nc.dram_tensor("wq", [E, E], F32, kind="ExternalInput").ap()
    hwk = nc.dram_tensor("wk", [E, 1024], F32, kind="ExternalInput").ap()
    hwv = nc.dram_tensor("wv", [E, 1024], F32, kind="ExternalInput").ap()
    hbq = nc.dram_tensor("bq2", [32, 128], F32, kind="ExternalInput").ap()
    hbk = nc.dram_tensor("bk2", [8, 128], F32, kind="ExternalInput").ap()
    hbv = nc.dram_tensor("bv", [1, 1024], F32, kind="ExternalInput").ap()
    hw1 = nc.dram_tensor("w1qT", [1024, HFF], F32, kind="ExternalInput").ap()
    hw3 = nc.dram_tensor("w3qT", [1024, HFF], F32, kind="ExternalInput").ap()
    hw2 = nc.dram_tensor("w2qT", [2752, E], F32, kind="ExternalInput").ap()
    ha1 = nc.dram_tensor("a1", [2, E], F32, kind="ExternalInput").ap()
    ha3 = nc.dram_tensor("a3", [2, E], F32, kind="ExternalInput").ap()
    ha2 = nc.dram_tensor("a2", [2, HFF], F32, kind="ExternalInput").ap()
    hb1 = nc.dram_tensor("b1T", [2, HFF], F32, kind="ExternalInput").ap()
    hb3 = nc.dram_tensor("b3T", [2, HFF], F32, kind="ExternalInput").ap()
    hb2 = nc.dram_tensor("b2T", [2, E], F32, kind="ExternalInput").ap()
    hcos = nc.dram_tensor("cosh", [128, TOK], F32, kind="ExternalInput").ap()
    hsin = nc.dram_tensor("sinh", [128, TOK], F32, kind="ExternalInput").ap()
    heyeb = nc.dram_tensor("eyebh", [128, 128], BF16, kind="ExternalInput").ap()
    heyef = nc.dram_tensor("eyefh", [32, 32], F32, kind="ExternalInput").ap()
    hcon = nc.dram_tensor("conh", [128, 8], F32, kind="ExternalInput").ap()
    hout = nc.dram_tensor("out", [TOK, E], F32, kind="ExternalOutput").ap()
    cc_in = nc.dram_tensor("cc_in", [2048, TOK], BF16)
    cc_out = nc.dram_tensor("cc_out", [8192, TOK], BF16)
    taps = {}
    tap_shapes = {"xnT": ([128, 32 * TOK], BF16), "qT": ([128, NH * TOK], BF16),
                  "cc": ([8192, TOK], BF16), "s": ([128, 8 * E], BF16),
                  "u1": ([2, TOK], BF16), "u3": ([2, TOK], BF16),
                  "xf": ([128, 8 * TOK], BF16), "yT": ([128, 4 * 512], BF16),
                  "g": ([128, 512], F32)}
    for t in TAPS:
        shp, dt = tap_shapes[t]
        taps[t] = nc.dram_tensor("tap_" + t, shp, dt, kind="ExternalOutput").ap()

    ctx = ExitStack()
    with ctx:
        En = ctx.enter_context
        r1 = En(nc.sbuf_tensor("r1", [128, 32 * TOK], BF16))
        r2 = En(nc.sbuf_tensor("r2", [128, 32 * TOK], BF16))
        xf = En(nc.sbuf_tensor("xf", [128, 8 * TOK], BF16))
        con = En(nc.sbuf_tensor("con", [128, 8], F32))
        stats = En(nc.sbuf_tensor("stats", [128, 64], F32))
        bqpp = En(nc.sbuf_tensor("bqpp", [128, 32], F32))
        bkpp = En(nc.sbuf_tensor("bkpp", [128, 8], F32))
        a2T = En(nc.sbuf_tensor("a2T", [128, 172], BF16))
        u1b = En(nc.sbuf_tensor("u1b", [2, TOK], BF16))
        u3b = En(nc.sbuf_tensor("u3b", [2, TOK], BF16))
        u2b = En(nc.sbuf_tensor("u2b", [2, 512], BF16))
        recb = En(nc.sbuf_tensor("recb", [128, 8], F32))
        lsb = En(nc.sbuf_tensor("lsb", [128, 4], F32))
        rtmp = En(nc.sbuf_tensor("rtmp", [128, TOK], F32))
        rtmp2 = En(nc.sbuf_tensor("rtmp2", [128, TOK], F32))
        ps = En(nc.psum_tensor("ps", [128, 4096], F32))

        PE = [0]
        ACT = [0]
        DVE = [0]
        PL = [0]
        hCC = En(nc.semaphore("sCC"))
        hPE = En(nc.semaphore("sPE"))
        hACT = En(nc.semaphore("sACT"))
        hDVE = En(nc.semaphore("sDVE"))
        hSC = En(nc.semaphore("sSC"))
        hEXP = En(nc.semaphore("sEXP"))
        hPV = En(nc.semaphore("sPV"))
        hEV = En(nc.semaphore("sEV"))
        hPL = En(nc.semaphore("sPL"))
        block = En(nc.Block())

        def pe_i(ins):
            PE[0] += 1
            ins.then_inc(hPE, 1)
            return PE[0]

        def act_i(ins):
            ACT[0] += 1
            ins.then_inc(hACT, 1)
            return ACT[0]

        def dve_i(ins):
            DVE[0] += 1
            ins.then_inc(hDVE, 1)
            return DVE[0]

        def pl_i(ins):
            PL[0] += 1
            ins.then_inc(hPL, 1)
            return PL[0]

        class Q:

            def __init__(self, name, R=8):
                self.sems = [En(nc.semaphore(f"{name}{i}")) for i in range(R)]
                self.cnt = [0] * R
                self.idx = 0
                self.R = R

            def dma(self, eng, **kw):
                r = self.idx % self.R
                if self.cnt[r]:
                    eng.wait_ge(self.sems[r], self.cnt[r])
                eng.dma_start(**kw).then_inc(self.sems[r], 16)
                self.cnt[r] += 16
                self.idx += 1
                return (r, self.cnt[r])

            def w(self, eng, tok):
                eng.wait_ge(self.sems[tok[0]], tok[1])

            def snap(self):
                return [(r, c) for r, c in enumerate(self.cnt) if c]

            def wsnap(self, eng, snap):
                for r, c in snap:
                    eng.wait_ge(self.sems[r], c)

        qL = Q("qL")
        qG = Q("qG")

        xnT = r1
        qT = r2

        xstage = [r2[:, 0:8192].bitcast(F32), r2[:, 8192:16384].bitcast(F32)]
        xnbf = [r2[:, 16384:20480], r2[:, 20480:24576]]
        sqscr = r2[:, 24576:28672]

        actx = ExitStack()
        An = actx.enter_context
        eyeb = An(nc.sbuf_tensor("eyeb", [128, 128], BF16))
        a1T = An(nc.sbuf_tensor("a1T", [128, 64], BF16))
        a3T = An(nc.sbuf_tensor("a3T", [128, 64], BF16))
        cosT = An(nc.sbuf_tensor("cosT", [128, TOK], F32))
        sinT = An(nc.sbuf_tensor("sinT", [128, TOK], F32))
        ast = An(nc.sbuf_tensor("ast", [2, 1024], F32))
        astb = An(nc.sbuf_tensor("astb", [2, 1024], BF16))
        kvctx = ExitStack()
        Kn = kvctx.enter_context
        eyef = Kn(nc.sbuf_tensor("eyef", [32, 32], F32))
        ones1 = Kn(nc.sbuf_tensor("ones1", [1, 128], BF16))
        browb = Kn(nc.sbuf_tensor("browb", [1, 1024], BF16))
        bqst = Kn(nc.sbuf_tensor("bqst", [32, 128], F32))
        bkst = Kn(nc.sbuf_tensor("bkst", [8, 128], F32))
        wslabp = [Kn(nc.sbuf_tensor(f"wsl{i}", [128, 1024], F32)) for i in range(2)]
        wslabbp = [Kn(nc.sbuf_tensor(f"wslb{i}", [128, 1024], BF16)) for i in range(2)]
        kb = [Kn(nc.sbuf_tensor(f"kb{i}", [128, TOK], BF16)) for i in range(2)]
        vst = [Kn(nc.sbuf_tensor(f"vst{i}", [128, 512], BF16)) for i in range(2)]
        brow = rtmp[0:1, 0:1024]

        def _(sync):
            for dst, srcap in [(con[:, :], hcon[:, :]), (eyeb[:, :], heyeb[:, :]),
                               (eyef[:, :], heyef[:, :]), (brow[0:1, :], hbv[0:1, :]),
                               (cosT[:, :], hcos[:, :]), (sinT[:, :], hsin[:, :]),
                               (bqst[:, :], hbq[:, :]), (bkst[:, :], hbk[:, :])]:
                qL.dma(sync, out=dst, in_=srcap)
        block.sync(_)
        ld_const = qL.snap()

        def _(v):
            qL.wsnap(v, ld_const)
            dve_i(v.memset(ones1[:, :], 1.0))
            dve_i(v.tensor_copy(out=browb[0:1, :], in_=brow[0:1, :]))
        block.vector(_)
        dv_misc = DVE[0]

        trq = ps[:, 3584:3616]
        trk = ps[:, 3616:3624]

        def _(t):
            qL.wsnap(t, ld_const)
            pe_i(t.transpose(out=trq[:, :], in_=bqst[:, :], identity=eyef[:, :]))
            pe_i(t.transpose(out=trk[:, :], in_=bkst[:, :], identity=eyef[0:8, 0:8]))
        block.tensor(_)
        pe_bt = PE[0]

        def _(v):
            v.wait_ge(hPE, pe_bt)
            dve_i(v.tensor_copy(out=bqpp[:, :], in_=trq[:, :]))
            dve_i(v.tensor_copy(out=bkpp[:, :], in_=trk[:, :]))
        block.vector(_)
        dv_bpp = DVE[0]

        ld_x = [0] * 8
        dv_norm = [0] * 8
        act_sq = [0] * 8
        pe_trlast = [0] * 8
        act_evac_end = [0]
        TRPS = [ps[:, 2048 + 64:2048 + 128], ps[:, 2560 + 64:2560 + 128],
                ps[:, 3072:3136], ps[:, 3584:3648]]
        CSUM, CSSQ, CM, CMSQ, CVAR, CLNV, CR, CNMR = 0, 8, 16, 24, 32, 40, 48, 56

        for t8 in range(8):
            def _(sync, t8=t8):
                if t8 >= 2:
                    sync.wait_ge(hDVE, dv_norm[t8 - 2])
                    sync.wait_ge(hACT, act_sq[t8 - 2])
                ld_x[t8] = qL.dma(sync,
                    out=xstage[t8 % 2][:, :],
                    in_=hx.rearrange("(t p) e -> t p e", p=128)[t8])
            block.sync(_)

            def _(v, t8=t8):
                qL.w(v, ld_x[t8])
                dve_i(v.reduce_sum(out=stats[:, CSUM + t8:CSUM + t8 + 1],
                                   in_=xstage[t8 % 2][:, :], axis=mybir.AxisListType.X))
            block.vector(_)
            dv_sum = DVE[0]

            def _(a, t8=t8):
                qL.w(a, ld_x[t8])
                if t8 >= 1:
                    a.wait_ge(hACT, act_sq[t8 - 1])
                act_sq[t8] = act_i(a.activation(
                    out=sqscr[:, :], in_=xstage[t8 % 2][:, :], func=AF.Square,
                    accum_out=stats[:, CSSQ + t8:CSSQ + t8 + 1]))
            block.scalar(_)

            def _(v, t8=t8, dv_sum=dv_sum):
                v.wait_ge(hDVE, dv_sum)
                c1 = dve_i(v.tensor_scalar(out=stats[:, CM + t8:CM + t8 + 1],
                                           in0=stats[:, CSUM + t8:CSUM + t8 + 1],
                                           scalar1=con[:, 4:5], scalar2=None, op0=AL.mult))
                v.wait_ge(hDVE, c1)
                c2 = dve_i(v.tensor_mul(out=stats[:, CMSQ + t8:CMSQ + t8 + 1],
                                        in0=stats[:, CM + t8:CM + t8 + 1],
                                        in1=stats[:, CM + t8:CM + t8 + 1]))
                v.wait_ge(hDVE, c2)
                v.wait_ge(hACT, act_sq[t8])
                dve_i(v.scalar_tensor_tensor(out=stats[:, CVAR + t8:CVAR + t8 + 1],
                                             in0=stats[:, CSSQ + t8:CSSQ + t8 + 1],
                                             scalar=con[:, 4:5],
                                             in1=stats[:, CMSQ + t8:CMSQ + t8 + 1],
                                             op0=AL.mult, op1=AL.subtract))
            block.vector(_)
            dv_var = DVE[0]

            def _(a, t8=t8, dv_var=dv_var):
                a.wait_ge(hDVE, dv_var)
                c1 = act_i(a.activation(out=stats[:, CLNV + t8:CLNV + t8 + 1],
                                        in_=stats[:, CVAR + t8:CVAR + t8 + 1],
                                        func=AF.Ln, bias=con[:, 2:3]))
                a.wait_ge(hACT, c1)
                act_i(a.activation(out=stats[:, CR + t8:CR + t8 + 1],
                                   in_=stats[:, CLNV + t8:CLNV + t8 + 1],
                                   func=AF.Exp, scale=con[:, 3:4]))
            block.scalar(_)
            act_r = ACT[0]

            def _(v, t8=t8, act_r=act_r):
                v.wait_ge(hACT, act_r)
                c1 = dve_i(v.tensor_scalar(out=stats[:, CNMR + t8:CNMR + t8 + 1],
                                           in0=stats[:, CM + t8:CM + t8 + 1],
                                           scalar1=stats[:, CR + t8:CR + t8 + 1],
                                           scalar2=con[:, 6:7], op0=AL.mult, op1=AL.mult))
                v.wait_ge(hDVE, c1)
                if t8 >= 2:
                    v.wait_ge(hPE, pe_trlast[t8 - 2])
                dv_norm[t8] = dve_i(v.tensor_scalar(
                    out=xnbf[t8 % 2][:, :], in0=xstage[t8 % 2][:, :],
                    scalar1=stats[:, CR + t8:CR + t8 + 1],
                    scalar2=stats[:, CNMR + t8:CNMR + t8 + 1],
                    op0=AL.mult, op1=AL.add))
            block.vector(_)

            tr_cnt = []

            def _(t, t8=t8, tr_cnt=tr_cnt, act_r=act_r):
                t.wait_ge(hDVE, dv_norm[t8])
                if t8 >= 1:
                    t.wait_ge(hACT, act_evac_end[0])
                for e in range(32):
                    buf = e % 4
                    if e >= 4:
                        if buf % 2 == 0:
                            t.wait_ge(hDVE, dv_norm[t8] + (e - 4) // 2 + 1)
                        else:
                            t.wait_ge(hACT, act_r + (e - 4) // 2 + 1)
                    trp = TRPS[buf].bitcast(BF16)
                    tr_cnt.append(pe_i(t.transpose(
                        out=trp[:, :], in_=xnbf[t8 % 2][:, e * 128:e * 128 + 128],
                        identity=eyeb[:, :])))
                pe_trlast[t8] = PE[0]
            block.tensor(_)

            def _(v, t8=t8, tr_cnt=tr_cnt):
                for e in range(0, 32, 2):
                    trp = TRPS[e % 4].bitcast(BF16)
                    v.wait_ge(hPE, tr_cnt[e])
                    dve_i(v.tensor_copy(
                        out=xnT[:, e * TOK + t8 * 128:e * TOK + t8 * 128 + 128], in_=trp[:, :]))
            block.vector(_)

            def _(a, t8=t8, tr_cnt=tr_cnt):
                for e in range(1, 32, 2):
                    trp = TRPS[e % 4].bitcast(BF16)
                    a.wait_ge(hPE, tr_cnt[e])
                    act_i(a.copy(
                        out=xnT[:, e * TOK + t8 * 128:e * TOK + t8 * 128 + 128], in_=trp[:, :]))
                act_evac_end[0] = ACT[0]
            block.scalar(_)
        xnT_dve, xnT_act = DVE[0], ACT[0]

        kb_dma = [None] * 8
        rope_prev = [0]
        rope_psum_done = 0
        vst_dma = [None] * 16
        ldw_t = [None]
        slab_cast = [0, 0]
        slab_used = [0, 0]
        for pas in range(2):
            for e in range(32):
                def _(sync, pas=pas, e=e):
                    if e % 2 == 0:
                        if slab_cast[(e // 2) % 2]:
                            sync.wait_ge(hDVE, slab_cast[(e // 2) % 2])
                        ldw_t[0] = (qL.dma(sync,
                            out=wslabp[(e // 2) % 2][:, :].rearrange("p (a c) -> p a c", c=512),
                            in_=hwk[e * 128:e * 128 + 256, pas * 512:pas * 512 + 512]
                                .rearrange("(a p) c -> p a c", p=128)),)
                block.sync(_)
                ldw = ldw_t[0]

                def _(v, e=e, ldw=ldw):
                    if e % 2 == 0:
                        for tk in ldw:
                            qL.w(v, tk)
                        if slab_used[(e // 2) % 2]:
                            v.wait_ge(hPE, slab_used[(e // 2) % 2])
                        slab_cast[(e // 2) % 2] = dve_i(v.tensor_copy(
                            out=wslabbp[(e // 2) % 2][:, :], in_=wslabp[(e // 2) % 2][:, :]))
                block.vector(_)

                def _(t, pas=pas, e=e):
                    t.wait_ge(hDVE, slab_cast[(e // 2) % 2])
                    if e == 0:
                        t.wait_ge(hDVE, xnT_dve)
                        t.wait_ge(hACT, xnT_act)
                        if pas == 1:
                            t.wait_ge(hDVE, rope_psum_done)
                    last = None
                    for kkl in range(4):
                        for half in range(2):
                            last = t.matmul(
                                out=ps[:, kkl * 1024 + half * 512:kkl * 1024 + half * 512 + 512],
                                lhsT=wslabbp[(e // 2) % 2][:, (e % 2) * 512 + kkl * 128:
                                                           (e % 2) * 512 + kkl * 128 + 128],
                                rhs=xnT[:, e * TOK + half * 512:e * TOK + half * 512 + 512],
                                start=(e == 0), stop=(e == 31))
                    if e % 2 == 1:
                        slab_used[(e // 2) % 2] = pe_i(last)
                block.tensor(_)
            pe_kpass = PE[0]

            for kkl in range(4):
                g = pas * 4 + kkl
                pbase = kkl * 1024

                def _(v, g=g, pbase=pbase, pe_kpass=pe_kpass):
                    v.wait_ge(hPE, pe_kpass)
                    if g == 0:
                        v.wait_ge(hDVE, dv_misc)
                    if rope_prev[0]:
                        v.wait_ge(hDVE, rope_prev[0])
                    if g >= 2:
                        qG.w(v, kb_dma[g - 2])
                    p = ps[:, pbase:pbase + 1024]
                    bl = bkpp[0:64, g:g + 1]
                    bh = bkpp[64:128, g:g + 1]
                    c1 = dve_i(v.scalar_tensor_tensor(out=rtmp[0:64, :], in0=p[64:128, :],
                                                      scalar=bh, in1=sinT[64:128, :],
                                                      op0=AL.add, op1=AL.mult))
                    c2 = dve_i(v.scalar_tensor_tensor(out=rtmp[64:128, :], in0=p[0:64, :],
                                                      scalar=bl, in1=sinT[0:64, :],
                                                      op0=AL.add, op1=AL.mult))
                    c3 = dve_i(v.scalar_tensor_tensor(out=rtmp2[0:64, :], in0=p[0:64, :],
                                                      scalar=bl, in1=cosT[0:64, :],
                                                      op0=AL.add, op1=AL.mult))
                    c4 = dve_i(v.scalar_tensor_tensor(out=rtmp2[64:128, :], in0=p[64:128, :],
                                                      scalar=bh, in1=cosT[64:128, :],
                                                      op0=AL.add, op1=AL.mult))
                    v.wait_ge(hDVE, c3)
                    dve_i(v.tensor_sub(out=kb[g % 2][0:64, :], in0=rtmp2[0:64, :],
                                       in1=rtmp[0:64, :]))
                    v.wait_ge(hDVE, c4)
                    dve_i(v.tensor_add(out=kb[g % 2][64:128, :], in0=rtmp2[64:128, :],
                                       in1=rtmp[64:128, :]))
                    rope_prev[0] = DVE[0]
                block.vector(_)
                if kkl == 3:
                    rope_psum_done = DVE[0]
                dv_kb = DVE[0]

                def _(gp, g=g, dv_kb=dv_kb):
                    gp.wait_ge(hDVE, dv_kb)
                    kb_dma[g] = qG.dma(gp,
                        out=cc_in.ap()[g * 128:g * 128 + 128, :], in_=kb[g % 2][:, :])
                block.gpsimd(_)
            if pas == 1:
                rope_psum_done = DVE[0]

        for pas in range(2):
            for e in range(32):
                def _(sync, pas=pas, e=e):
                    if e % 2 == 0:
                        if slab_cast[(e // 2) % 2]:
                            sync.wait_ge(hDVE, slab_cast[(e // 2) % 2])
                        ldw_t[0] = (qL.dma(sync,
                            out=wslabp[(e // 2) % 2][:, :].rearrange("p (a c) -> p a c", c=512),
                            in_=hwv[e * 128:e * 128 + 256, pas * 512:pas * 512 + 512]
                                .rearrange("(a p) c -> p a c", p=128)),)
                block.sync(_)
                ldw = ldw_t[0]

                def _(v, e=e, ldw=ldw):
                    if e % 2 == 0:
                        for tk in ldw:
                            qL.w(v, tk)
                        if slab_used[(e // 2) % 2]:
                            v.wait_ge(hPE, slab_used[(e // 2) % 2])
                        slab_cast[(e // 2) % 2] = dve_i(v.tensor_copy(
                            out=wslabbp[(e // 2) % 2][:, :], in_=wslabp[(e // 2) % 2][:, :]))
                block.vector(_)

                def _(t, pas=pas, e=e):
                    t.wait_ge(hDVE, slab_cast[(e // 2) % 2])
                    if e == 0:
                        if pas == 0:
                            t.wait_ge(hDVE, rope_psum_done)
                        else:
                            t.wait_ge(hACT, vevac_all)
                    last = None
                    for tt in range(8):
                        last = t.matmul(out=ps[:, tt * 512:tt * 512 + 512],
                                        lhsT=xnT[:, e * TOK + tt * 128:e * TOK + tt * 128 + 128],
                                        rhs=wslabbp[(e // 2) % 2][:, (e % 2) * 512:(e % 2) * 512 + 512],
                                        start=(e == 0), stop=False)
                    if e % 2 == 1:
                        slab_used[(e // 2) % 2] = pe_i(last)
                block.tensor(_)

            def _(t, pas=pas):
                t.wait_ge(hDVE, dv_misc)
                last = None
                for tt in range(8):
                    last = t.matmul(out=ps[:, tt * 512:tt * 512 + 512],
                                    lhsT=ones1[0:1, :],
                                    rhs=browb[0:1, pas * 512:pas * 512 + 512],
                                    start=False, stop=True)
                pe_i(last)
            block.tensor(_)
            pe_vpass = PE[0]

            for tt in range(8):
                def _(a, tt=tt, pas=pas, pe_vpass=pe_vpass):
                    a.wait_ge(hPE, pe_vpass)
                    if pas * 8 + tt >= 2:
                        qG.w(a, vst_dma[pas * 8 + tt - 2])
                    act_i(a.copy(out=vst[tt % 2][:, :], in_=ps[:, tt * 512:tt * 512 + 512]))
                block.scalar(_)
                a_ev = ACT[0]

                def _(gp, tt=tt, pas=pas, a_ev=a_ev):
                    gp.wait_ge(hACT, a_ev)
                    vst_dma[pas * 8 + tt] = qG.dma(gp,
                        out=cc_in.ap()[1024 + tt * 128:1024 + tt * 128 + 128,
                                       pas * 512:pas * 512 + 512],
                        in_=vst[tt % 2][:, :])
                block.gpsimd(_)
            vevac_all = ACT[0]

        def _(gp):
            qG.wsnap(gp, qG.snap())
            if SIM_LOCAL:
                for r in range(4):
                    qG.dma(gp, out=cc_out.ap()[r * 2048:(r + 1) * 2048, :],
                           in_=cc_in.ap()[:, :])
                qG.wsnap(gp, qG.snap())
                gp.sem_inc(hCC, 1)
            else:
                gp.collective_compute(
                    "AllGather", AL.bypass, replica_groups=[[0, 1, 2, 3], [4, 5, 6, 7]],
                    ins=[cc_in.ap().opt()], outs=[cc_out.ap().opt()]).then_inc(hCC, 1)
        block.gpsimd(_)

        aT_ev = [0, 0]
        ast_used_dve = 0
        astb_used_pe = 0
        a_eidx = [0]
        for (hsrc, dstT, ntile) in [(ha1, a1T, 32), (ha3, a3T, 32), (ha2, a2T, NHT)]:
            nch = (ntile + 7) // 8
            for ch in range(nch):
                e0, e1 = ch * 8, min(ch * 8 + 8, ntile)
                ncol = (e1 - e0) * 128

                def _(sync, hsrc=hsrc, e0=e0, ncol=ncol):
                    if ast_used_dve:
                        sync.wait_ge(hDVE, ast_used_dve)
                    ldw_t[0] = qL.dma(sync, out=ast[0:2, 0:ncol],
                                      in_=hsrc[0:2, e0 * 128:e0 * 128 + ncol])
                block.sync(_)
                lda = ldw_t[0]

                def _(v, lda=lda, ncol=ncol):
                    qL.w(v, lda)
                    if astb_used_pe:
                        v.wait_ge(hPE, astb_used_pe)
                    dve_i(v.tensor_copy(out=astb[0:2, 0:ncol], in_=ast[0:2, 0:ncol]))
                block.vector(_)
                ast_used_dve = DVE[0]
                cast_c = DVE[0]

                for e in range(e0, e1):
                    abuf = a_eidx[0] % 2
                    a_eidx[0] += 1
                    trp2 = (ps[:, 3072:3073] if abuf == 0 else ps[:, 3584:3585]).bitcast(BF16)

                    def _(t, cast_c=cast_c, trp2=trp2, e=e, e0=e0, abuf=abuf):
                        t.wait_ge(hDVE, cast_c)
                        t.wait_ge(hACT, vevac_all)
                        if aT_ev[abuf]:
                            t.wait_ge(hDVE, aT_ev[abuf])
                        pe_i(t.transpose(out=trp2[:, :],
                                         in_=astb[0:2, (e - e0) * 128:(e - e0) * 128 + 128],
                                         identity=eyeb[0:2, 0:2]))
                    block.tensor(_)
                    trc = PE[0]

                    def _(v, dstT=dstT, e=e, trc=trc, trp2=trp2):
                        v.wait_ge(hPE, trc)
                        dve_i(v.tensor_copy(out=dstT[:, 2 * e:2 * e + 2], in_=trp2[:, :]))
                    block.vector(_)
                    aT_ev[abuf] = DVE[0]
                astb_used_pe = PE[0]

        u_evac_prev = [0]
        for (aT, udst) in [(a1T, u1b), (a3T, u3b)]:
            def _(t, aT=aT):
                t.wait_ge(hDVE, max(aT_ev))
                if u_evac_prev[0]:
                    t.wait_ge(hDVE, u_evac_prev[0])
                last = None
                for e in range(32):
                    for half in range(2):
                        last = t.matmul(out=ps[0:2, 2048 + half * 512:2048 + half * 512 + 512],
                                        lhsT=aT[:, 2 * e:2 * e + 2],
                                        rhs=xnT[:, e * TOK + half * 512:e * TOK + half * 512 + 512],
                                        start=(e == 0), stop=(e == 31))
                pe_i(last)
            block.tensor(_)
            umm = PE[0]

            def _(v, udst=udst, umm=umm):
                v.wait_ge(hPE, umm)
                dve_i(v.tensor_scalar(out=udst[0:2, :], in0=ps[0:2, 2048:3072],
                                      scalar1=con[0:2, 5:6], scalar2=None, op0=AL.mult))
            block.vector(_)
            u_evac_prev[0] = DVE[0]
        u_dve = DVE[0]
        kvctx.close()

        wqh = [An(nc.sbuf_tensor(f"wqh{i}", [128, 2048], F32)) for i in range(2)]
        wqb = [An(nc.sbuf_tensor(f"wqb{i}", [128, E], BF16)) for i in range(2)]
        A3_PE, A3_DVE = PE[0], DVE[0]
        qcast = [0, 0]
        qmm = [0, 0]
        qrope = [0, 0]
        wqh_cast = [0, 0]

        def emit_q_load(h):
            for half in range(2):
                def _(sync, h=h, half=half):
                    if h == 0 and half == 0:
                        qG.wsnap(sync, qG.snap())
                        sync.wait_ge(hPE, A3_PE)
                        sync.wait_ge(hDVE, A3_DVE)
                    if wqh_cast[half]:
                        sync.wait_ge(hDVE, wqh_cast[half])
                    ldw_t[0] = qL.dma(sync,
                        out=wqh[half][:, :].rearrange("p (e d) -> p e d", d=128),
                        in_=hwq.rearrange("(e p) d -> p e d", p=128)
                        [:, half * 16:half * 16 + 16, h * 128:h * 128 + 128])
                block.sync(_)
                ldq = ldw_t[0]

                def _(v, h=h, half=half, ldq=ldq):
                    qL.w(v, ldq)
                    if qmm[h % 2]:
                        v.wait_ge(hPE, qmm[h % 2])
                    if h == 0 and half == 0:
                        v.wait_ge(hDVE, u_dve)
                    wqh_cast[half] = dve_i(v.tensor_copy(
                        out=wqb[h % 2][:, half * 2048:half * 2048 + 2048], in_=wqh[half][:, :]))
                block.vector(_)
            qcast[h % 2] = DVE[0]

        emit_q_load(0)
        for h in range(NH):
            if h + 1 < NH:
                emit_q_load(h + 1)

            pbase = (h % 2) * 1024

            def _(t, h=h, pbase=pbase):
                t.wait_ge(hDVE, qcast[h % 2])
                if qrope[h % 2]:
                    t.wait_ge(hDVE, qrope[h % 2])
                if h == 0:
                    t.wait_ge(hDVE, u_dve)
                last = None
                for e in range(32):
                    for half in range(2):
                        last = t.matmul(
                            out=ps[:, pbase + half * 512:pbase + half * 512 + 512],
                            lhsT=wqb[h % 2][:, e * 128:e * 128 + 128],
                            rhs=xnT[:, e * TOK + half * 512:e * TOK + half * 512 + 512],
                            start=(e == 0), stop=(e == 31))
                qmm[h % 2] = pe_i(last)
            block.tensor(_)

            def _(v, h=h, pbase=pbase):
                v.wait_ge(hPE, qmm[h % 2])
                if rope_prev[0]:
                    v.wait_ge(hDVE, rope_prev[0])
                p = ps[:, pbase:pbase + 1024]
                bl = bqpp[0:64, h:h + 1]
                bh = bqpp[64:128, h:h + 1]
                dst = qT[:, h * TOK:(h + 1) * TOK]
                c1 = dve_i(v.scalar_tensor_tensor(out=rtmp[0:64, :], in0=p[64:128, :],
                                                  scalar=bh, in1=sinT[64:128, :],
                                                  op0=AL.add, op1=AL.mult))
                c2 = dve_i(v.scalar_tensor_tensor(out=rtmp[64:128, :], in0=p[0:64, :],
                                                  scalar=bl, in1=sinT[0:64, :],
                                                  op0=AL.add, op1=AL.mult))
                c3 = dve_i(v.scalar_tensor_tensor(out=rtmp2[0:64, :], in0=p[0:64, :],
                                                  scalar=bl, in1=cosT[0:64, :],
                                                  op0=AL.add, op1=AL.mult))
                c4 = dve_i(v.scalar_tensor_tensor(out=rtmp2[64:128, :], in0=p[64:128, :],
                                                  scalar=bh, in1=cosT[64:128, :],
                                                  op0=AL.add, op1=AL.mult))
                v.wait_ge(hDVE, c3)
                dve_i(v.tensor_sub(out=dst[0:64, :], in0=rtmp2[0:64, :], in1=rtmp[0:64, :]))
                v.wait_ge(hDVE, c4)
                dve_i(v.tensor_add(out=dst[64:128, :], in0=rtmp2[64:128, :], in1=rtmp[64:128, :]))
                qrope[h % 2] = DVE[0]
                rope_prev[0] = DVE[0]
            block.vector(_)
        qT_dve = DVE[0]
        qproj_pe = PE[0]

        def _(v):
            v.wait_ge(hDVE, qT_dve)
            for ep in range(8):
                if ep > 0:
                    v.wait_ge(hDVE, DVE[0])
                c1 = dve_i(v.tensor_add(out=rtmp[:, :].bitcast(BF16)[:, 0:1024],
                                        in0=xnT[:, ep * TOK:(ep + 1) * TOK],
                                        in1=xnT[:, (ep + 8) * TOK:(ep + 9) * TOK]))
                c2 = dve_i(v.tensor_add(out=rtmp2[:, :].bitcast(BF16)[:, 0:1024],
                                        in0=xnT[:, (ep + 16) * TOK:(ep + 17) * TOK],
                                        in1=xnT[:, (ep + 24) * TOK:(ep + 25) * TOK]))
                v.wait_ge(hDVE, c2)
                dve_i(v.tensor_add(out=xf[:, ep * TOK:(ep + 1) * TOK],
                                   in0=rtmp[:, :].bitcast(BF16)[:, 0:1024],
                                   in1=rtmp2[:, :].bitcast(BF16)[:, 0:1024]))
        block.vector(_)
        xf_dve = DVE[0]

        def _(gp):
            if "xnT" in TAPS:
                gp.wait_ge(hDVE, xnT_dve)
                gp.wait_ge(hACT, xnT_act)
                qG.dma(gp, out=taps["xnT"], in_=xnT[:, :])
            if "qT" in TAPS:
                gp.wait_ge(hDVE, qT_dve)
                qG.dma(gp, out=taps["qT"], in_=qT[:, :])
            if "cc" in TAPS:
                gp.wait_ge(hCC, 1)
                qG.dma(gp, out=taps["cc"], in_=cc_out.ap()[:, :])
            if "u1" in TAPS:
                gp.wait_ge(hDVE, u_dve)
                qG.dma(gp, out=taps["u1"], in_=u1b[0:2, :])
            if "u3" in TAPS:
                gp.wait_ge(hDVE, u_dve)
                qG.dma(gp, out=taps["u3"], in_=u3b[0:2, :])
            if "xf" in TAPS:
                gp.wait_ge(hDVE, xf_dve)
                qG.dma(gp, out=taps["xf"], in_=xf[:, :])
        block.gpsimd(_)

        actx.close()

        bctx = ExitStack()
        Bn = bctx.enter_context
        kTb = [Bn(nc.sbuf_tensor(f"kT{i}", [128, 4096], BF16)) for i in range(2)]
        vab = [Bn(nc.sbuf_tensor(f"va{i}", [128, 32 * 129], BF16)) for i in range(2)]
        pt = Bn(nc.sbuf_tensor("pt", [128, 4 * 1024], BF16))
        s_sb = r1

        def _(v):
            v.wait_ge(hDVE, xf_dve)
            dve_i(v.memset(vab[0][:, :], 1.0))
            dve_i(v.memset(vab[1][:, :], 1.0))
        block.vector(_)
        ones_dve = DVE[0]

        kv_tok = [[] for _ in range(8)]

        def _(gp):
            for g in range(8):
                if g == 0:
                    gp.wait_ge(hCC, 1)
                    gp.wait_ge(hDVE, ones_dve)
                    gp.wait_ge(hPE, qproj_pe)
                if g >= 2:
                    gp.wait_ge(hSC, (g - 1) * 128)
                    gp.wait_ge(hPV, (g - 1) * 128)
                kv_tok[g].append(qG.dma(gp,
                    out=kTb[g % 2][:, :].rearrange("p (r t) -> p r t", r=4),
                    in_=cc_out.ap().rearrange("(r q p) t -> p r q t", p=128, q=16)[:, :, g, :]))
                for r in range(4):
                    kv_tok[g].append(qG.dma(gp,
                        out=vab[g % 2][:, :].rearrange("p (t c) -> p t c", c=129)
                            [:, r * 8:r * 8 + 8, 0:128],
                        in_=cc_out.ap()[r * 2048 + 1024:r * 2048 + 2048,
                                        g * 128:g * 128 + 128]
                            .rearrange("(k p) d -> p k d", p=128)))
        block.gpsimd(_)

        NJ = 64 * 16
        DPH = 9
        sc_n = [0]
        pv_n = [0]

        def b_scores(t, J):
            hh, j = J // 16, J % 16
            g = hh // 8
            h = g * 4 + (hh // 2) % 4
            half = hh % 2
            if j == 0:
                for tok in kv_tok[g]:
                    qG.w(t, tok)
            if J >= 2:
                t.wait_ge(hEXP, J - 1)
            for e2 in range(2):
                kt = 2 * j + e2
                ins = t.matmul(
                    out=ps[:, (J % 2) * 1024 + e2 * 512:(J % 2) * 1024 + e2 * 512 + 512],
                    lhsT=kTb[g % 2][:, kt * 128:kt * 128 + 128],
                    rhs=qT[:, h * TOK + half * 512:h * TOK + half * 512 + 512],
                    start=True, stop=True)
                if e2 == 1:
                    sc_n[0] += 1
                    ins.then_inc(hSC, 1)

        def b_pv(t, J):
            hh, j = J // 16, J % 16
            g = hh // 8
            pb = 2048 + (hh % 2) * 1024
            t.wait_ge(hEXP, J + 1)
            if j == 0 and hh >= 2:
                t.wait_ge(hEV, (hh - 1) * DPH)
            for e2 in range(2):
                kt = 2 * j + e2
                for qt in range(4):
                    off = pb + (qt // 2) * 512 + (qt % 2) * 129
                    ins = t.matmul(
                        out=ps[:, off:off + 129],
                        lhsT=pt[:, (J % 4) * 1024 + e2 * 512 + qt * 128:
                                (J % 4) * 1024 + e2 * 512 + qt * 128 + 128],
                        rhs=vab[g % 2][:, kt * 129:kt * 129 + 129],
                        start=(j == 0 and e2 == 0 and qt % 2 == 0),
                        stop=(j == 15 and e2 == 1 and qt % 2 == 1))
                    if e2 == 1 and qt == 3:
                        pv_n[0] += 1
                        ins.then_inc(hPV, 1)

        def _(t):
            b_scores(t, 0)
            b_scores(t, 1)
            for J in range(NJ):
                if J + 2 < NJ:
                    b_scores(t, J + 2)
                b_pv(t, J)
        block.tensor(_)

        def _(a):
            qL.wsnap(a, ld_const)
            for J in range(NJ):
                a.wait_ge(hSC, J + 1)
                if J >= 4:
                    a.wait_ge(hPV, J - 3)
                a.activation(out=pt[:, (J % 4) * 1024:(J % 4) * 1024 + 1024],
                             in_=ps[:, (J % 2) * 1024:(J % 2) * 1024 + 1024],
                             func=AF.Exp, bias=con[:, 0:1], scale=con[:, 1:2]).then_inc(hEXP, 1)
        block.scalar(_)

        ev_n = [0]

        def _(v):
            v.wait_ge(hACT, xnT_act)
            v.wait_ge(hDVE, xf_dve)
            for hh in range(64):
                g = hh // 8
                h = g * 4 + (hh // 2) % 4
                half = hh % 2
                pb = 2048 + (hh % 2) * 1024
                base = (hh % 2) * 4
                v.wait_ge(hPV, (hh + 1) * 16)
                if hh >= 1:
                    v.wait_ge(hEV, ev_n[0])
                for qt in range(4):
                    off = pb + (qt // 2) * 512 + (qt % 2) * 129 + 128
                    v.tensor_copy(out=lsb[:, qt:qt + 1], in_=ps[:, off:off + 1]).then_inc(hEV, 1)
                    ev_n[0] += 1
                v.wait_ge(hEV, ev_n[0])
                v.reciprocal(out=recb[:, base:base + 4], in_=lsb[:, 0:4]).then_inc(hEV, 1)
                ev_n[0] += 1
                rcp = ev_n[0]
                for qt in range(4):
                    tq = half * 4 + qt
                    v.wait_ge(hEV, rcp)
                    v.tensor_scalar(
                        out=s_sb[:, tq * E + h * 128:tq * E + h * 128 + 128],
                        in0=ps[:, pb + (qt // 2) * 512 + (qt % 2) * 129:
                               pb + (qt // 2) * 512 + (qt % 2) * 129 + 128],
                        scalar1=recb[:, base + qt:base + qt + 1], scalar2=None,
                        op0=AL.mult).then_inc(hEV, 1)
                    ev_n[0] += 1
        block.vector(_)
        B_EV = 64 * DPH

        def _(gp):
            if "s" in TAPS:
                gp.wait_ge(hEV, B_EV)
                qG.dma(gp, out=taps["s"], in_=s_sb[:, :])
        block.gpsimd(_)

        bctx.close()

        cctx = ExitStack()
        Cn = cctx.enter_context
        yTov = Cn(nc.sbuf_tensor("yTov", [128, (NHT - 64) * 512], BF16))
        w1f = [Cn(nc.sbuf_tensor(f"w1f{i}", [128, 8 * 128], F32)) for i in range(2)]
        w3f = [Cn(nc.sbuf_tensor(f"w3f{i}", [128, 8 * 128], F32)) for i in range(2)]
        w1bb = [Cn(nc.sbuf_tensor(f"w1b{i}", [128, 8 * 128], BF16)) for i in range(2)]
        w3bb = [Cn(nc.sbuf_tensor(f"w3b{i}", [128, 8 * 128], BF16)) for i in range(2)]
        bsl = Cn(nc.sbuf_tensor("bsl", [2, 512], F32))
        bs_all = Cn(nc.sbuf_tensor("bs_all", [2, 1024], BF16))
        bslb = [bs_all[0:2, 0:256], bs_all[0:2, 256:512]]
        ysig = [rtmp2[:, 0:512], rtmp2[:, 512:1024]]
        w2fp = [rtmp[:, 0:1024], w3f[0]]
        b2sl = bsl
        b2slb = bs_all[0:2, 512:1024]
        xrl = [w1f[0][:, 0:512], w1f[0][:, 512:1024]]
        ost = [w1f[1][:, 0:512], w1f[1][:, 512:1024]]
        yta = rtmp[:, 0:256].bitcast(BF16)
        ytb = rtmp[:, 256:512].bitcast(BF16)

        def yt(ht):
            if ht < 64:
                return r2[:, ht * 512:ht * 512 + 512]
            return yTov[:, (ht - 64) * 512:(ht - 64) * 512 + 512]

        lin2_pe_done = 0
        g_tap_tok = [None]
        prev_ymul_end = [0]
        last_lin2_act = [0]
        lin2_gd_end = [[]]
        lin2_dve_end = [0]
        lin2_pl_end = [0]
        for tc in range(2):
            wcast1 = [0, 0]
            wcast3 = [0, 0]
            wused = [0, 0]
            bcast = [0, 0]
            bused = [0, 0]
            ymul_l = [0] * NHT
            mm13_l = [0] * NHT
            bcast_last = [0]
            ysig_war = [0]
            yT_dve = [0] * NHT
            actC0 = ACT[0]
            silu1 = lambda ht: actC0 + ht + 1

            def emit_w13_load(tc, ht):
                def _(sync, tc=tc, ht=ht):
                    if tc == 0 and ht == 0:
                        sync.wait_ge(hSC, NJ)
                        sync.wait_ge(hPV, NJ)
                        sync.wait_ge(hEXP, NJ)
                        sync.wait_ge(hEV, B_EV)
                    if tc == 1 and ht <= 1:
                        qG.wsnap(sync, lin2_gd_end[0])
                        sync.wait_ge(hDVE, lin2_dve_end[0])
                        sync.wait_ge(hPL, lin2_pl_end[0])
                    if wcast1[ht % 2]:
                        sync.wait_ge(hDVE, wcast1[ht % 2])
                    t1 = qL.dma(sync,
                        out=w1f[ht % 2][:, :].rearrange("p (k h) -> p k h", h=128),
                        in_=hw1.rearrange("(k p) h -> p k h", p=128)[:, :, ht * 128:ht * 128 + 128])
                    if wcast3[ht % 2]:
                        sync.wait_ge(hDVE, wcast3[ht % 2])
                    t2 = qL.dma(sync,
                        out=w3f[ht % 2][:, :].rearrange("p (k h) -> p k h", h=128),
                        in_=hw3.rearrange("(k p) h -> p k h", p=128)[:, :, ht * 128:ht * 128 + 128])
                    if bcast_last[0]:
                        sync.wait_ge(hDVE, bcast_last[0])
                    t3 = qL.dma(sync, out=bsl[0:2, 0:128],
                                in_=hb1[0:2, ht * 128:ht * 128 + 128])
                    t4 = qL.dma(sync, out=bsl[0:2, 128:256],
                                in_=hb3[0:2, ht * 128:ht * 128 + 128])
                    ldw_t[0] = (t1, t2, t3, t4)
                block.sync(_)
                lds = ldw_t[0]

                def _(v, tc=tc, ht=ht, lds=lds):
                    for tk in lds:
                        qL.w(v, tk)
                    if tc == 1 and ht <= 1 and lin2_pe_done:
                        v.wait_ge(hPE, lin2_pe_done)
                    if wused[ht % 2]:
                        v.wait_ge(hPE, wused[ht % 2])
                    dve_i(v.tensor_copy(out=w1bb[ht % 2][:, :], in_=w1f[ht % 2][:, :]))
                    wcast1[ht % 2] = DVE[0]
                    dve_i(v.tensor_copy(out=w3bb[ht % 2][:, :], in_=w3f[ht % 2][:, :]))
                    wcast3[ht % 2] = DVE[0]
                    if bused[ht % 2]:
                        v.wait_ge(hPE, bused[ht % 2])
                    dve_i(v.tensor_copy(out=bslb[ht % 2][0:2, :], in_=bsl[0:2, 0:256]))
                    bcast[ht % 2] = DVE[0]
                    bcast_last[0] = DVE[0]
                block.vector(_)

            emit_w13_load(tc, 0)
            for ht in range(NHT):
                cur_cast = max(wcast1[ht % 2], wcast3[ht % 2], bcast[ht % 2])
                if ht + 1 < NHT:
                    emit_w13_load(tc, ht + 1)

                p1 = ps[:, (ht % 2) * 512:(ht % 2) * 512 + 512]
                p3 = ps[:, 1024 + (ht % 2) * 512:1024 + (ht % 2) * 512 + 512]

                def _(t, tc=tc, ht=ht, cur_cast=cur_cast, p1=p1, p3=p3):
                    t.wait_ge(hDVE, cur_cast)
                    if tc == 0 and ht == 0:
                        t.wait_ge(hEV, B_EV)
                        t.wait_ge(hEXP, NJ)
                        t.wait_ge(hDVE, xf_dve)
                    if tc == 1 and ht == 0:
                        t.wait_ge(hACT, last_lin2_act[0])
                    if ht >= 2:
                        t.wait_ge(hACT, silu1(ht - 2))
                        t.wait_ge(hDVE, ymul_l[ht - 2])
                    for k in range(8):
                        t.matmul(out=p1[:, :], lhsT=w1bb[ht % 2][:, k * 128:k * 128 + 128],
                                 rhs=xf[:, k * TOK + tc * 512:k * TOK + tc * 512 + 512],
                                 start=(k == 0), stop=False)
                    t.matmul(out=p1[:, :], lhsT=bslb[ht % 2][0:2, 0:128],
                             rhs=u1b[0:2, tc * 512:tc * 512 + 512], start=False, stop=True)
                    for k in range(8):
                        t.matmul(out=p3[:, :], lhsT=w3bb[ht % 2][:, k * 128:k * 128 + 128],
                                 rhs=xf[:, k * TOK + tc * 512:k * TOK + tc * 512 + 512],
                                 start=(k == 0), stop=False)
                    last = t.matmul(out=p3[:, :], lhsT=bslb[ht % 2][0:2, 128:256],
                                    rhs=u3b[0:2, tc * 512:tc * 512 + 512], start=False, stop=True)
                    pe_i(last)
                block.tensor(_)
                mm13_l[ht] = PE[0]
                wused[ht % 2] = PE[0]
                bused[ht % 2] = PE[0]

                def _(v, tc=tc, ht=ht, p3=p3):
                    v.wait_ge(hACT, silu1(ht))
                    if tc == 1 and ht < NYF and lin2_pe_done:
                        v.wait_ge(hPE, lin2_pe_done)
                    dve_i(v.tensor_mul(out=yt(ht), in0=ysig[ht % 2][:, :], in1=p3[:, :]))
                block.vector(_)
                ymul_l[ht] = DVE[0]
                ysig_war[0] = DVE[0]
                yT_dve[ht] = DVE[0]

                if ht >= 1:
                    hp = ht - 1

                    def _(t, hp=hp):
                        t.wait_ge(hDVE, yT_dve[hp])
                        t.matmul(out=ps[0:2, 2048:2560], lhsT=a2T[:, 2 * hp:2 * hp + 2],
                                 rhs=yt(hp)[0:128, :], start=(hp == 0), stop=False)
                    block.tensor(_)

            def _(a, tc=tc, mm13_l=mm13_l, ymul_l=ymul_l, actC0=actC0):
                for ht in range(NHT):
                    a.wait_ge(hPE, mm13_l[ht])
                    if ht >= 2:
                        a.wait_ge(hDVE, ymul_l[ht - 2])
                    elif tc == 1 and ht < 2 and prev_ymul_end[0]:
                        a.wait_ge(hDVE, prev_ymul_end[0])
                    p1 = ps[:, (ht % 2) * 512:(ht % 2) * 512 + 512]
                    act_i(a.activation(out=ysig[ht % 2][:, :], in_=p1[:, :], func=AF.Silu))
            block.scalar(_)
            assert ACT[0] == silu1(NHT - 1)

            def _(t):
                t.wait_ge(hDVE, yT_dve[NHT - 1])
                pe_i(t.matmul(out=ps[0:2, 2048:2560], lhsT=a2T[:, 2 * (NHT - 1):2 * (NHT - 1) + 2],
                              rhs=yt(NHT - 1)[0:128, :], start=False, stop=True))
            block.tensor(_)
            u2mm = PE[0]

            def _(v, u2mm=u2mm):
                v.wait_ge(hPE, u2mm)
                if lin2_pe_done:
                    v.wait_ge(hPE, lin2_pe_done)
                dve_i(v.tensor_scalar(out=u2b[0:2, :], in0=ps[0:2, 2048:2560],
                                      scalar1=con[0:2, 5:6], scalar2=None, op0=AL.mult))
            block.vector(_)
            u2_dve = DVE[0]

            def _(v, yT_dve=yT_dve):
                v.wait_ge(hDVE, yT_dve[NHT - 1])
                for f in range(NYF):
                    if f > 0:
                        v.wait_ge(hDVE, DVE[0])
                    if f < 21:
                        c1 = dve_i(v.tensor_add(out=yta[:, :], in0=yt(f), in1=yt(f + 43)))
                        c2 = dve_i(v.tensor_add(out=ytb[0:64, :],
                                                in0=yt(f + 21)[64:128, :],
                                                in1=yt(f + 64)[64:128, :]))
                        c3 = dve_i(v.tensor_add(out=ytb[64:128, :],
                                                in0=yt(f + 22)[0:64, :],
                                                in1=yt(f + 65)[0:64, :]))
                        v.wait_ge(hDVE, c3)
                        dve_i(v.tensor_add(out=yt(f), in0=yta[:, :], in1=ytb[:, :]))
                    else:
                        c1 = dve_i(v.tensor_add(out=yta[0:64, :], in0=yt(21)[0:64, :],
                                                in1=yt(64)[0:64, :]))
                        c2 = dve_i(v.tensor_add(out=ytb[0:64, :], in0=yt(42)[64:128, :],
                                                in1=yt(85)[64:128, :]))
                        v.wait_ge(hDVE, c2)
                        dve_i(v.tensor_add(out=yt(21)[0:64, :], in0=yta[0:64, :],
                                           in1=ytb[0:64, :]))
            block.vector(_)
            yf_dve = DVE[0]

            if "yT" in TAPS and tc == 0:
                def _(gp):
                    gp.wait_ge(hDVE, yf_dve)
                    qG.dma(gp, out=taps["yT"], in_=r2[:, 0:4 * 512])
                block.gpsimd(_)

            w2cast = [0, 0]
            w2used = [0, 0]
            b2cast = 0
            b2used = 0
            xld = [None, None]
            xuse = [0, 0]
            ouse = [None, None]
            ost_dma = [None, None]
            mmlin2_l = [0] * 8
            mult2_l = [0] * 32
            ysig_war0 = ysig_war[0]
            actC1 = ACT[0]
            silu2 = lambda i: actC1 + i + 1

            def emit_combine(tc, ec):
                for tt in range(4):
                    tq = tc * 4 + tt
                    i2 = ec * 4 + tt
                    buf = i2 % 2

                    def _(sync, tc=tc, ec=ec, tt=tt, tq=tq, buf=buf):
                        if ec == 0 and tt == 0:
                            sync.wait_ge(hDVE, wcast1[0])
                            sync.wait_ge(hDVE, wcast1[1])
                        if xuse[buf]:
                            sync.wait_ge(hPL, xuse[buf])
                        xld[buf] = qL.dma(sync,
                            out=xrl[buf][:, :],
                            in_=hx[tq * 128:tq * 128 + 128, ec * 512:ec * 512 + 512])
                    block.sync(_)

                    def _(gp, tc=tc, ec=ec, tt=tt, tq=tq, buf=buf, i2=i2):
                        gp.wait_ge(hACT, silu2(i2))
                        if ouse[buf]:
                            qG.w(gp, ouse[buf])
                        c1 = pl_i(gp.tensor_mul(out=ost[buf][:, :], in0=ysig[buf][:, :],
                                                in1=s_sb[:, tq * E + ec * 512:tq * E + ec * 512 + 512]))
                        mult2_l[i2] = c1
                        qL.w(gp, xld[buf])
                        gp.wait_ge(hPL, c1)
                        pl_i(gp.tensor_add(out=ost[buf][:, :], in0=ost[buf][:, :],
                                           in1=xrl[buf][:, :]))
                        xuse[buf] = PL[0]
                        gp.wait_ge(hPL, xuse[buf])
                        ost_dma[buf] = qG.dma(gp,
                            out=hout[tq * 128:tq * 128 + 128, ec * 512:ec * 512 + 512],
                            in_=ost[buf][:, :])
                        ouse[buf] = ost_dma[buf]
                    block.gpsimd(_)

            for ec in range(8):
                def _(sync, tc=tc, ec=ec):
                    if ec == 0:
                        sync.wait_ge(hDVE, bcast[0])
                        sync.wait_ge(hDVE, bcast[1])
                    if b2cast:
                        sync.wait_ge(hDVE, b2cast)
                    ldw_t[0] = qL.dma(sync, out=b2sl[0:2, 0:512],
                                      in_=hb2[0:2, ec * 512:ec * 512 + 512])
                block.sync(_)
                ldb2 = ldw_t[0]

                def _(v, ldb2=ldb2):
                    qL.w(v, ldb2)
                    if b2used:
                        v.wait_ge(hPE, b2used)
                    dve_i(v.tensor_copy(out=b2slb[0:2, :], in_=b2sl[0:2, 0:512]))
                block.vector(_)
                b2cast = DVE[0]

                if ec >= 1:
                    emit_combine(tc, ec - 1)

                for j in range(11):
                    def _(sync, tc=tc, ec=ec, j=j):
                        if ec == 0 and j == 0:
                            sync.wait_ge(hDVE, yf_dve)
                        if ec == 0 and j == 1:
                            sync.wait_ge(hDVE, wcast3[0])
                            sync.wait_ge(hDVE, wcast3[1])
                        if w2cast[j % 2]:
                            sync.wait_ge(hDVE, w2cast[j % 2])
                        if j < 10:
                            ldw_t[0] = (qL.dma(sync,
                                out=w2fp[j % 2][:, :].rearrange("p (a c) -> p a c", c=512),
                                in_=hw2[j * 256:j * 256 + 256, ec * 512:ec * 512 + 512]
                                    .rearrange("(a p) c -> p a c", p=128)),)
                        else:
                            t1 = qL.dma(sync, out=w2fp[j % 2][:, 0:512],
                                        in_=hw2[2560:2688, ec * 512:ec * 512 + 512])
                            t2 = qL.dma(sync, out=w2fp[j % 2][0:64, 512:1024],
                                        in_=hw2[2688:2752, ec * 512:ec * 512 + 512])
                            ldw_t[0] = (t1, t2)
                    block.sync(_)
                    ldw2 = ldw_t[0]

                    def _(v, j=j, ldw2=ldw2):
                        for tk in ldw2:
                            qL.w(v, tk)
                        if w2used[j % 2]:
                            v.wait_ge(hPE, w2used[j % 2])
                        else:
                            v.wait_ge(hPE, mm13_l[NHT - 1])
                        dve_i(v.tensor_copy(out=w1bb[j % 2][:, :], in_=w2fp[j % 2][:, :]))
                    block.vector(_)
                    w2cast[j % 2] = DVE[0]
                    castc = DVE[0]

                    def _(t, tc=tc, ec=ec, j=j, castc=castc):
                        t.wait_ge(hDVE, castc)
                        if j == 0:
                            t.wait_ge(hDVE, u2_dve)
                            if ec >= 1:
                                t.wait_ge(hACT, silu2((ec - 1) * 4 + 3))
                        last = None
                        for k in (2 * j, 2 * j + 1):
                            kr = 128 if k < 21 else 64
                            for tt in range(4):
                                last = t.matmul(
                                    out=ps[:, tt * 512:tt * 512 + 512],
                                    lhsT=yt(k)[0:kr, tt * 128:tt * 128 + 128],
                                    rhs=w1bb[j % 2][0:kr, (k % 2) * 512:(k % 2) * 512 + 512],
                                    start=(k == 0), stop=False)
                        if j == 10:
                            for tt in range(4):
                                last = t.matmul(
                                    out=ps[:, tt * 512:tt * 512 + 512],
                                    lhsT=u2b[0:2, tt * 128:tt * 128 + 128],
                                    rhs=b2slb[0:2, :], start=False, stop=True)
                        pe_i(last)
                    block.tensor(_)
                    w2used[j % 2] = PE[0]
                    if j == 10:
                        b2used = PE[0]
                mmlin2_l[ec] = PE[0]

            emit_combine(tc, 7)

            def _(a, tc=tc, mmlin2_l=mmlin2_l, mult2_l=mult2_l, ysig_war0=ysig_war0):
                for ec in range(8):
                    a.wait_ge(hPE, mmlin2_l[ec])
                    for tt in range(4):
                        i2 = ec * 4 + tt
                        if i2 >= 2:
                            a.wait_ge(hPL, mult2_l[i2 - 2])
                        else:
                            a.wait_ge(hDVE, ysig_war0)
                        act_i(a.activation(out=ysig[i2 % 2][:, :],
                                           in_=ps[:, tt * 512:tt * 512 + 512], func=AF.Silu))
            block.scalar(_)
            assert ACT[0] == silu2(31)
            prev_ymul_end[0] = DVE[0]
            lin2_pe_done = PE[0]
            last_lin2_act[0] = ACT[0]
            lin2_gd_end[0] = qG.snap()
            lin2_dve_end[0] = DVE[0]
            lin2_pl_end[0] = PL[0]

        def _(gp):
            qG.wsnap(gp, qG.snap())
            qL.wsnap(gp, qL.snap())
        block.gpsimd(_)

        cctx.close()
    return nc


def _host_inputs(inputs, core):
    b, c = core // 4, core % 4
    x = np.ascontiguousarray(np.asarray(inputs["L0_attn_x"])[b, c * 1024:(c + 1) * 1024, :])
    d = {"x": x}
    d["wq"] = _W["wq"]
    d["wk"] = _W["wk"]
    d["wv"] = _W["wv"]
    d["bq2"] = _W["bq2"]
    d["bk2"] = _W["bk2"]
    d["bv"] = _W["bv"]
    d["w1qT"] = _W["w1qT"]
    d["w3qT"] = _W["w3qT"]
    d["w2qT"] = _W["w2qT"]
    d["a1"] = _W["a1"]
    d["a3"] = _W["a3"]
    d["a2"] = _W["a2"]
    d["b1T"] = _W["b1T"]
    d["b3T"] = _W["b3T"]
    d["b2T"] = _W["b2T"]
    d["cosh"] = _ROPE[0][c]
    d["sinh"] = _ROPE[1][c]
    d["eyebh"] = np.eye(128, dtype=ml_dtypes.bfloat16)
    d["eyefh"] = np.eye(32, dtype=np.float32)
    d["conh"] = _CON
    return d


_W = {}
_ROPE = None
_CON = np.tile(np.array([[SHIFT, SCALE, EPS, -0.5, 1.0 / 4096.0, 0.5, -1.0, 1.0]],
                        dtype=np.float32), (128, 1))


def kernel(**inputs):
    global _ROPE
    asnp = lambda a: np.ascontiguousarray(np.asarray(a, dtype=np.float32))
    _W["wq"] = asnp(inputs["L0_attn_wq"])
    _W["wk"] = asnp(inputs["L0_attn_wk"])
    _W["wv"] = asnp(inputs["L0_attn_wv"])
    _W["bq2"] = asnp(inputs["L0_attn_bq"]).reshape(32, 128)
    _W["bk2"] = asnp(inputs["L0_attn_bk"]).reshape(8, 128)
    _W["bv"] = asnp(inputs["L0_attn_bv"]).reshape(1, 1024)
    _W["w1qT"] = asnp(np.asarray(inputs["L0_ffn_w1q"]).T)
    _W["w3qT"] = asnp(np.asarray(inputs["L0_ffn_w3q"]).T)
    _W["w2qT"] = asnp(np.asarray(inputs["L0_ffn_w2q"]).T)
    _W["a1"] = asnp(inputs["L0_ffn_a1"])
    _W["a3"] = asnp(inputs["L0_ffn_a3"])
    _W["a2"] = asnp(inputs["L0_ffn_a2"])
    _W["b1T"] = asnp(np.asarray(inputs["L0_ffn_b1"]).T)
    _W["b3T"] = asnp(np.asarray(inputs["L0_ffn_b3"]).T)
    _W["b2T"] = asnp(np.asarray(inputs["L0_ffn_b2"]).T)
    inv = (1.0 / 10000.0 ** (np.arange(0, 128, 2, dtype=np.float32) / np.float32(128.0))
           ).astype(np.float32)
    pos = np.arange(4096, dtype=np.float32)
    ang = pos[:, None] * inv[None, :]
    cos = np.cos(ang).astype(np.float32)
    sin = np.sin(ang).astype(np.float32)
    cosC = [np.ascontiguousarray(np.tile(cos[c * 1024:(c + 1) * 1024].T, (2, 1)))
            for c in range(4)]
    sinC = [np.ascontiguousarray(np.tile(sin[c * 1024:(c + 1) * 1024].T, (2, 1)))
            for c in range(4)]
    _ROPE = (cosC, sinC)

    nc = build_nc()
    in_maps = [_host_inputs(inputs, core) for core in range(8)]
    res = run_bass_kernel_spmd(nc, in_maps, core_ids=list(range(8)))
    out = np.empty((2, 4096, 4096), dtype=np.float32)
    for core in range(8):
        b, c = core // 4, core % 4
        out[b, c * 1024:(c + 1) * 1024, :] = res.results[core]["out"]
    return out


# version 11
# speedup vs baseline: 1.1223x; 1.1223x over previous
import numpy as np
import ml_dtypes
import concourse.bass as bass
import concourse.mybir as mybir
from concourse.bass_utils import run_bass_kernel_spmd
from contextlib import ExitStack

F32 = mybir.dt.float32
BF16 = mybir.dt.bfloat16
AF = mybir.ActivationFunctionType
AL = mybir.AluOpType

TOK = 1024
E = 4096
NH = 32
D = 128
HFF = 11008
NHT = 86
NYF = 22
SCALE = float(1.0 / np.sqrt(128.0))
SHIFT = -12.0
EPS = 1e-5

TAPS = ()
SIM_LOCAL = False


def build_nc():
    nc = bass.Bass("TRN2", target_bir_lowering=False)
    hx = nc.dram_tensor("x", [TOK, E], F32, kind="ExternalInput").ap()
    hwq = nc.dram_tensor("wq", [E, E], F32, kind="ExternalInput").ap()
    hwk = nc.dram_tensor("wk", [E, 1024], F32, kind="ExternalInput").ap()
    hwv = nc.dram_tensor("wv", [E, 1024], F32, kind="ExternalInput").ap()
    hbq = nc.dram_tensor("bq2", [32, 128], F32, kind="ExternalInput").ap()
    hbk = nc.dram_tensor("bk2", [8, 128], F32, kind="ExternalInput").ap()
    hbv = nc.dram_tensor("bv", [1, 1024], F32, kind="ExternalInput").ap()
    hw1 = nc.dram_tensor("w1qT", [1024, HFF], F32, kind="ExternalInput").ap()
    hw3 = nc.dram_tensor("w3qT", [1024, HFF], F32, kind="ExternalInput").ap()
    hw2 = nc.dram_tensor("w2qT", [2752, E], F32, kind="ExternalInput").ap()
    ha1 = nc.dram_tensor("a1", [2, E], F32, kind="ExternalInput").ap()
    ha3 = nc.dram_tensor("a3", [2, E], F32, kind="ExternalInput").ap()
    ha2 = nc.dram_tensor("a2", [2, HFF], F32, kind="ExternalInput").ap()
    hb1 = nc.dram_tensor("b1T", [2, HFF], F32, kind="ExternalInput").ap()
    hb3 = nc.dram_tensor("b3T", [2, HFF], F32, kind="ExternalInput").ap()
    hb2 = nc.dram_tensor("b2T", [2, E], F32, kind="ExternalInput").ap()
    hcos = nc.dram_tensor("cosh", [128, TOK], F32, kind="ExternalInput").ap()
    hsin = nc.dram_tensor("sinh", [128, TOK], F32, kind="ExternalInput").ap()
    heyeb = nc.dram_tensor("eyebh", [128, 128], BF16, kind="ExternalInput").ap()
    heyef = nc.dram_tensor("eyefh", [32, 32], F32, kind="ExternalInput").ap()
    hcon = nc.dram_tensor("conh", [128, 8], F32, kind="ExternalInput").ap()
    hout = nc.dram_tensor("out", [TOK, E], F32, kind="ExternalOutput").ap()
    cc_in = nc.dram_tensor("cc_in", [2048, TOK], BF16)
    cc_out = nc.dram_tensor("cc_out", [8192, TOK], BF16)
    taps = {}
    tap_shapes = {"xnT": ([128, 32 * TOK], BF16), "qT": ([128, NH * TOK], BF16),
                  "cc": ([8192, TOK], BF16), "s": ([128, 8 * E], BF16),
                  "u1": ([2, TOK], BF16), "u3": ([2, TOK], BF16),
                  "xf": ([128, 8 * TOK], BF16), "yT": ([128, 4 * 512], BF16),
                  "g": ([128, 512], F32)}
    for t in TAPS:
        shp, dt = tap_shapes[t]
        taps[t] = nc.dram_tensor("tap_" + t, shp, dt, kind="ExternalOutput").ap()

    ctx = ExitStack()
    with ctx:
        En = ctx.enter_context
        r1 = En(nc.sbuf_tensor("r1", [128, 32 * TOK], BF16))
        r2 = En(nc.sbuf_tensor("r2", [128, 32 * TOK], BF16))
        xf = En(nc.sbuf_tensor("xf", [128, 8 * TOK], BF16))
        con = En(nc.sbuf_tensor("con", [128, 8], F32))
        stats = En(nc.sbuf_tensor("stats", [128, 64], F32))
        bqpp = En(nc.sbuf_tensor("bqpp", [128, 32], F32))
        bkpp = En(nc.sbuf_tensor("bkpp", [128, 8], F32))
        a2T = En(nc.sbuf_tensor("a2T", [128, 172], BF16))
        u1b = En(nc.sbuf_tensor("u1b", [2, TOK], BF16))
        u3b = En(nc.sbuf_tensor("u3b", [2, TOK], BF16))
        u2b = En(nc.sbuf_tensor("u2b", [2, 512], BF16))
        recb = En(nc.sbuf_tensor("recb", [128, 8], F32))
        lsb = En(nc.sbuf_tensor("lsb", [128, 4], F32))
        rtmp = En(nc.sbuf_tensor("rtmp", [128, TOK], F32))
        rtmp2 = En(nc.sbuf_tensor("rtmp2", [128, TOK], F32))
        ps = En(nc.psum_tensor("ps", [128, 4096], F32))

        PE = [0]
        ACT = [0]
        DVE = [0]
        hCC = En(nc.semaphore("sCC"))
        hPE = En(nc.semaphore("sPE"))
        hACT = En(nc.semaphore("sACT"))
        hDVE = En(nc.semaphore("sDVE"))
        hSC = En(nc.semaphore("sSC"))
        hEXP = En(nc.semaphore("sEXP"))
        hPV = En(nc.semaphore("sPV"))
        hEV = En(nc.semaphore("sEV"))
        block = En(nc.Block())

        def pe_i(ins):
            PE[0] += 1
            ins.then_inc(hPE, 1)
            return PE[0]

        def act_i(ins):
            ACT[0] += 1
            ins.then_inc(hACT, 1)
            return ACT[0]

        def dve_i(ins):
            DVE[0] += 1
            ins.then_inc(hDVE, 1)
            return DVE[0]

        class Q:

            def __init__(self, name, R=8):
                self.sems = [En(nc.semaphore(f"{name}{i}")) for i in range(R)]
                self.cnt = [0] * R
                self.idx = 0
                self.R = R

            def dma(self, eng, **kw):
                r = self.idx % self.R
                if self.cnt[r]:
                    eng.wait_ge(self.sems[r], self.cnt[r])
                eng.dma_start(**kw).then_inc(self.sems[r], 16)
                self.cnt[r] += 16
                self.idx += 1
                return (r, self.cnt[r])

            def w(self, eng, tok):
                eng.wait_ge(self.sems[tok[0]], tok[1])

            def snap(self):
                return [(r, c) for r, c in enumerate(self.cnt) if c]

            def wsnap(self, eng, snap):
                for r, c in snap:
                    eng.wait_ge(self.sems[r], c)

        qL = Q("qL")
        qG = Q("qG")

        xnT = r1
        qT = r2

        xstage = [r2[:, 0:8192].bitcast(F32), r2[:, 8192:16384].bitcast(F32)]
        xnbf = [r2[:, 16384:20480], r2[:, 20480:24576]]
        sqscr = r2[:, 24576:28672]

        actx = ExitStack()
        An = actx.enter_context
        eyeb = An(nc.sbuf_tensor("eyeb", [128, 128], BF16))
        a1T = An(nc.sbuf_tensor("a1T", [128, 64], BF16))
        a3T = An(nc.sbuf_tensor("a3T", [128, 64], BF16))
        cosT = An(nc.sbuf_tensor("cosT", [128, TOK], F32))
        sinT = An(nc.sbuf_tensor("sinT", [128, TOK], F32))
        ast = An(nc.sbuf_tensor("ast", [2, 1024], F32))
        astb = An(nc.sbuf_tensor("astb", [2, 1024], BF16))
        kvctx = ExitStack()
        Kn = kvctx.enter_context
        eyef = Kn(nc.sbuf_tensor("eyef", [32, 32], F32))
        ones1 = Kn(nc.sbuf_tensor("ones1", [1, 128], BF16))
        browb = Kn(nc.sbuf_tensor("browb", [1, 1024], BF16))
        bqst = Kn(nc.sbuf_tensor("bqst", [32, 128], F32))
        bkst = Kn(nc.sbuf_tensor("bkst", [8, 128], F32))
        wslabp = [Kn(nc.sbuf_tensor(f"wsl{i}", [128, 1024], F32)) for i in range(2)]
        wslabbp = [Kn(nc.sbuf_tensor(f"wslb{i}", [128, 1024], BF16)) for i in range(2)]
        kb = [Kn(nc.sbuf_tensor(f"kb{i}", [128, TOK], BF16)) for i in range(2)]
        vst = [Kn(nc.sbuf_tensor(f"vst{i}", [128, 512], BF16)) for i in range(2)]
        brow = rtmp[0:1, 0:1024]

        def _(sync):
            for dst, srcap in [(con[:, :], hcon[:, :]), (eyeb[:, :], heyeb[:, :]),
                               (eyef[:, :], heyef[:, :]), (brow[0:1, :], hbv[0:1, :]),
                               (cosT[:, :], hcos[:, :]), (sinT[:, :], hsin[:, :]),
                               (bqst[:, :], hbq[:, :]), (bkst[:, :], hbk[:, :])]:
                qL.dma(sync, out=dst, in_=srcap)
        block.sync(_)
        ld_const = qL.snap()

        def _(v):
            qL.wsnap(v, ld_const)
            dve_i(v.memset(ones1[:, :], 1.0))
            dve_i(v.tensor_copy(out=browb[0:1, :], in_=brow[0:1, :]))
        block.vector(_)
        dv_misc = DVE[0]

        trq = ps[:, 3584:3616]
        trk = ps[:, 3616:3624]

        def _(t):
            qL.wsnap(t, ld_const)
            pe_i(t.transpose(out=trq[:, :], in_=bqst[:, :], identity=eyef[:, :]))
            pe_i(t.transpose(out=trk[:, :], in_=bkst[:, :], identity=eyef[0:8, 0:8]))
        block.tensor(_)
        pe_bt = PE[0]

        def _(v):
            v.wait_ge(hPE, pe_bt)
            dve_i(v.tensor_copy(out=bqpp[:, :], in_=trq[:, :]))
            dve_i(v.tensor_copy(out=bkpp[:, :], in_=trk[:, :]))
        block.vector(_)
        dv_bpp = DVE[0]

        ld_x = [0] * 8
        dv_norm = [0] * 8
        act_sq = [0] * 8
        pe_trlast = [0] * 8
        act_evac_end = [0]
        TRPS = [ps[:, 2048 + 64:2048 + 128], ps[:, 2560 + 64:2560 + 128],
                ps[:, 3072:3136], ps[:, 3584:3648]]
        CSUM, CSSQ, CM, CMSQ, CVAR, CLNV, CR, CNMR = 0, 8, 16, 24, 32, 40, 48, 56

        for t8 in range(8):
            def _(sync, t8=t8):
                if t8 >= 2:
                    sync.wait_ge(hDVE, dv_norm[t8 - 2])
                    sync.wait_ge(hACT, act_sq[t8 - 2])
                ld_x[t8] = qL.dma(sync,
                    out=xstage[t8 % 2][:, :],
                    in_=hx.rearrange("(t p) e -> t p e", p=128)[t8])
            block.sync(_)

            def _(v, t8=t8):
                qL.w(v, ld_x[t8])
                dve_i(v.reduce_sum(out=stats[:, CSUM + t8:CSUM + t8 + 1],
                                   in_=xstage[t8 % 2][:, :], axis=mybir.AxisListType.X))
            block.vector(_)
            dv_sum = DVE[0]

            def _(a, t8=t8):
                qL.w(a, ld_x[t8])
                if t8 >= 1:
                    a.wait_ge(hACT, act_sq[t8 - 1])
                act_sq[t8] = act_i(a.activation(
                    out=sqscr[:, :], in_=xstage[t8 % 2][:, :], func=AF.Square,
                    accum_out=stats[:, CSSQ + t8:CSSQ + t8 + 1]))
            block.scalar(_)

            def _(v, t8=t8, dv_sum=dv_sum):
                v.wait_ge(hDVE, dv_sum)
                c1 = dve_i(v.tensor_scalar(out=stats[:, CM + t8:CM + t8 + 1],
                                           in0=stats[:, CSUM + t8:CSUM + t8 + 1],
                                           scalar1=con[:, 4:5], scalar2=None, op0=AL.mult))
                v.wait_ge(hDVE, c1)
                c2 = dve_i(v.tensor_mul(out=stats[:, CMSQ + t8:CMSQ + t8 + 1],
                                        in0=stats[:, CM + t8:CM + t8 + 1],
                                        in1=stats[:, CM + t8:CM + t8 + 1]))
                v.wait_ge(hDVE, c2)
                v.wait_ge(hACT, act_sq[t8])
                dve_i(v.scalar_tensor_tensor(out=stats[:, CVAR + t8:CVAR + t8 + 1],
                                             in0=stats[:, CSSQ + t8:CSSQ + t8 + 1],
                                             scalar=con[:, 4:5],
                                             in1=stats[:, CMSQ + t8:CMSQ + t8 + 1],
                                             op0=AL.mult, op1=AL.subtract))
            block.vector(_)
            dv_var = DVE[0]

            def _(a, t8=t8, dv_var=dv_var):
                a.wait_ge(hDVE, dv_var)
                c1 = act_i(a.activation(out=stats[:, CLNV + t8:CLNV + t8 + 1],
                                        in_=stats[:, CVAR + t8:CVAR + t8 + 1],
                                        func=AF.Ln, bias=con[:, 2:3]))
                a.wait_ge(hACT, c1)
                act_i(a.activation(out=stats[:, CR + t8:CR + t8 + 1],
                                   in_=stats[:, CLNV + t8:CLNV + t8 + 1],
                                   func=AF.Exp, scale=con[:, 3:4]))
            block.scalar(_)
            act_r = ACT[0]

            def _(v, t8=t8, act_r=act_r):
                v.wait_ge(hACT, act_r)
                c1 = dve_i(v.tensor_scalar(out=stats[:, CNMR + t8:CNMR + t8 + 1],
                                           in0=stats[:, CM + t8:CM + t8 + 1],
                                           scalar1=stats[:, CR + t8:CR + t8 + 1],
                                           scalar2=con[:, 6:7], op0=AL.mult, op1=AL.mult))
                v.wait_ge(hDVE, c1)
                if t8 >= 2:
                    v.wait_ge(hPE, pe_trlast[t8 - 2])
                dv_norm[t8] = dve_i(v.tensor_scalar(
                    out=xnbf[t8 % 2][:, :], in0=xstage[t8 % 2][:, :],
                    scalar1=stats[:, CR + t8:CR + t8 + 1],
                    scalar2=stats[:, CNMR + t8:CNMR + t8 + 1],
                    op0=AL.mult, op1=AL.add))
            block.vector(_)

            tr_cnt = []

            def _(t, t8=t8, tr_cnt=tr_cnt, act_r=act_r):
                t.wait_ge(hDVE, dv_norm[t8])
                if t8 >= 1:
                    t.wait_ge(hACT, act_evac_end[0])
                for e in range(32):
                    buf = e % 4
                    if e >= 4:
                        if buf % 2 == 0:
                            t.wait_ge(hDVE, dv_norm[t8] + (e - 4) // 2 + 1)
                        else:
                            t.wait_ge(hACT, act_r + (e - 4) // 2 + 1)
                    trp = TRPS[buf].bitcast(BF16)
                    tr_cnt.append(pe_i(t.transpose(
                        out=trp[:, :], in_=xnbf[t8 % 2][:, e * 128:e * 128 + 128],
                        identity=eyeb[:, :])))
                pe_trlast[t8] = PE[0]
            block.tensor(_)

            def _(v, t8=t8, tr_cnt=tr_cnt):
                for e in range(0, 32, 2):
                    trp = TRPS[e % 4].bitcast(BF16)
                    v.wait_ge(hPE, tr_cnt[e])
                    dve_i(v.tensor_copy(
                        out=xnT[:, e * TOK + t8 * 128:e * TOK + t8 * 128 + 128], in_=trp[:, :]))
            block.vector(_)

            def _(a, t8=t8, tr_cnt=tr_cnt):
                for e in range(1, 32, 2):
                    trp = TRPS[e % 4].bitcast(BF16)
                    a.wait_ge(hPE, tr_cnt[e])
                    act_i(a.copy(
                        out=xnT[:, e * TOK + t8 * 128:e * TOK + t8 * 128 + 128], in_=trp[:, :]))
                act_evac_end[0] = ACT[0]
            block.scalar(_)
        xnT_dve, xnT_act = DVE[0], ACT[0]

        kb_dma = [None] * 8
        rope_prev = [0]
        rope_psum_done = 0
        vst_dma = [None] * 16
        ldw_t = [None]
        slab_cast = [0, 0]
        slab_used = [0, 0]
        for pas in range(2):
            for e in range(32):
                def _(sync, pas=pas, e=e):
                    if e % 2 == 0:
                        if slab_cast[(e // 2) % 2]:
                            sync.wait_ge(hDVE, slab_cast[(e // 2) % 2])
                        ldw_t[0] = (qL.dma(sync,
                            out=wslabp[(e // 2) % 2][:, :].rearrange("p (a c) -> p a c", c=512),
                            in_=hwk[e * 128:e * 128 + 256, pas * 512:pas * 512 + 512]
                                .rearrange("(a p) c -> p a c", p=128)),)
                block.sync(_)
                ldw = ldw_t[0]

                def _(v, e=e, ldw=ldw):
                    if e % 2 == 0:
                        for tk in ldw:
                            qL.w(v, tk)
                        if slab_used[(e // 2) % 2]:
                            v.wait_ge(hPE, slab_used[(e // 2) % 2])
                        slab_cast[(e // 2) % 2] = dve_i(v.tensor_copy(
                            out=wslabbp[(e // 2) % 2][:, :], in_=wslabp[(e // 2) % 2][:, :]))
                block.vector(_)

                def _(t, pas=pas, e=e):
                    t.wait_ge(hDVE, slab_cast[(e // 2) % 2])
                    if e == 0:
                        t.wait_ge(hDVE, xnT_dve)
                        t.wait_ge(hACT, xnT_act)
                        if pas == 1:
                            t.wait_ge(hDVE, rope_psum_done)
                    last = None
                    for kkl in range(4):
                        for half in range(2):
                            last = t.matmul(
                                out=ps[:, kkl * 1024 + half * 512:kkl * 1024 + half * 512 + 512],
                                lhsT=wslabbp[(e // 2) % 2][:, (e % 2) * 512 + kkl * 128:
                                                           (e % 2) * 512 + kkl * 128 + 128],
                                rhs=xnT[:, e * TOK + half * 512:e * TOK + half * 512 + 512],
                                start=(e == 0), stop=(e == 31))
                    if e % 2 == 1:
                        slab_used[(e // 2) % 2] = pe_i(last)
                block.tensor(_)
            pe_kpass = PE[0]

            for kkl in range(4):
                g = pas * 4 + kkl
                pbase = kkl * 1024

                def _(v, g=g, pbase=pbase, pe_kpass=pe_kpass):
                    v.wait_ge(hPE, pe_kpass)
                    if g == 0:
                        v.wait_ge(hDVE, dv_misc)
                    if rope_prev[0]:
                        v.wait_ge(hDVE, rope_prev[0])
                    if g >= 2:
                        qG.w(v, kb_dma[g - 2])
                    p = ps[:, pbase:pbase + 1024]
                    bl = bkpp[0:64, g:g + 1]
                    bh = bkpp[64:128, g:g + 1]
                    c1 = dve_i(v.scalar_tensor_tensor(out=rtmp[0:64, :], in0=p[64:128, :],
                                                      scalar=bh, in1=sinT[64:128, :],
                                                      op0=AL.add, op1=AL.mult))
                    c2 = dve_i(v.scalar_tensor_tensor(out=rtmp[64:128, :], in0=p[0:64, :],
                                                      scalar=bl, in1=sinT[0:64, :],
                                                      op0=AL.add, op1=AL.mult))
                    c3 = dve_i(v.scalar_tensor_tensor(out=rtmp2[0:64, :], in0=p[0:64, :],
                                                      scalar=bl, in1=cosT[0:64, :],
                                                      op0=AL.add, op1=AL.mult))
                    c4 = dve_i(v.scalar_tensor_tensor(out=rtmp2[64:128, :], in0=p[64:128, :],
                                                      scalar=bh, in1=cosT[64:128, :],
                                                      op0=AL.add, op1=AL.mult))
                    v.wait_ge(hDVE, c3)
                    dve_i(v.tensor_sub(out=kb[g % 2][0:64, :], in0=rtmp2[0:64, :],
                                       in1=rtmp[0:64, :]))
                    v.wait_ge(hDVE, c4)
                    dve_i(v.tensor_add(out=kb[g % 2][64:128, :], in0=rtmp2[64:128, :],
                                       in1=rtmp[64:128, :]))
                    rope_prev[0] = DVE[0]
                block.vector(_)
                if kkl == 3:
                    rope_psum_done = DVE[0]
                dv_kb = DVE[0]

                def _(gp, g=g, dv_kb=dv_kb):
                    gp.wait_ge(hDVE, dv_kb)
                    kb_dma[g] = qG.dma(gp,
                        out=cc_in.ap()[g * 128:g * 128 + 128, :], in_=kb[g % 2][:, :])
                block.gpsimd(_)
            if pas == 1:
                rope_psum_done = DVE[0]

        for pas in range(2):
            for e in range(32):
                def _(sync, pas=pas, e=e):
                    if e % 2 == 0:
                        if slab_cast[(e // 2) % 2]:
                            sync.wait_ge(hDVE, slab_cast[(e // 2) % 2])
                        ldw_t[0] = (qL.dma(sync,
                            out=wslabp[(e // 2) % 2][:, :].rearrange("p (a c) -> p a c", c=512),
                            in_=hwv[e * 128:e * 128 + 256, pas * 512:pas * 512 + 512]
                                .rearrange("(a p) c -> p a c", p=128)),)
                block.sync(_)
                ldw = ldw_t[0]

                def _(v, e=e, ldw=ldw):
                    if e % 2 == 0:
                        for tk in ldw:
                            qL.w(v, tk)
                        if slab_used[(e // 2) % 2]:
                            v.wait_ge(hPE, slab_used[(e // 2) % 2])
                        slab_cast[(e // 2) % 2] = dve_i(v.tensor_copy(
                            out=wslabbp[(e // 2) % 2][:, :], in_=wslabp[(e // 2) % 2][:, :]))
                block.vector(_)

                def _(t, pas=pas, e=e):
                    t.wait_ge(hDVE, slab_cast[(e // 2) % 2])
                    if e == 0:
                        if pas == 0:
                            t.wait_ge(hDVE, rope_psum_done)
                        else:
                            t.wait_ge(hACT, vevac_all)
                    last = None
                    for tt in range(8):
                        last = t.matmul(out=ps[:, tt * 512:tt * 512 + 512],
                                        lhsT=xnT[:, e * TOK + tt * 128:e * TOK + tt * 128 + 128],
                                        rhs=wslabbp[(e // 2) % 2][:, (e % 2) * 512:(e % 2) * 512 + 512],
                                        start=(e == 0), stop=False)
                    if e % 2 == 1:
                        slab_used[(e // 2) % 2] = pe_i(last)
                block.tensor(_)

            def _(t, pas=pas):
                t.wait_ge(hDVE, dv_misc)
                last = None
                for tt in range(8):
                    last = t.matmul(out=ps[:, tt * 512:tt * 512 + 512],
                                    lhsT=ones1[0:1, :],
                                    rhs=browb[0:1, pas * 512:pas * 512 + 512],
                                    start=False, stop=True)
                pe_i(last)
            block.tensor(_)
            pe_vpass = PE[0]

            for tt in range(8):
                def _(a, tt=tt, pas=pas, pe_vpass=pe_vpass):
                    a.wait_ge(hPE, pe_vpass)
                    if pas * 8 + tt >= 2:
                        qG.w(a, vst_dma[pas * 8 + tt - 2])
                    act_i(a.copy(out=vst[tt % 2][:, :], in_=ps[:, tt * 512:tt * 512 + 512]))
                block.scalar(_)
                a_ev = ACT[0]

                def _(gp, tt=tt, pas=pas, a_ev=a_ev):
                    gp.wait_ge(hACT, a_ev)
                    vst_dma[pas * 8 + tt] = qG.dma(gp,
                        out=cc_in.ap()[1024 + tt * 128:1024 + tt * 128 + 128,
                                       pas * 512:pas * 512 + 512],
                        in_=vst[tt % 2][:, :])
                block.gpsimd(_)
            vevac_all = ACT[0]

        def _(gp):
            qG.wsnap(gp, qG.snap())
            if SIM_LOCAL:
                for r in range(4):
                    qG.dma(gp, out=cc_out.ap()[r * 2048:(r + 1) * 2048, :],
                           in_=cc_in.ap()[:, :])
                qG.wsnap(gp, qG.snap())
                gp.sem_inc(hCC, 1)
            else:
                gp.collective_compute(
                    "AllGather", AL.bypass, replica_groups=[[0, 1, 2, 3], [4, 5, 6, 7]],
                    ins=[cc_in.ap().opt()], outs=[cc_out.ap().opt()]).then_inc(hCC, 1)
        block.gpsimd(_)

        aT_ev = [0, 0]
        ast_used_dve = 0
        astb_used_pe = 0
        a_eidx = [0]
        for (hsrc, dstT, ntile) in [(ha1, a1T, 32), (ha3, a3T, 32), (ha2, a2T, NHT)]:
            nch = (ntile + 7) // 8
            for ch in range(nch):
                e0, e1 = ch * 8, min(ch * 8 + 8, ntile)
                ncol = (e1 - e0) * 128

                def _(sync, hsrc=hsrc, e0=e0, ncol=ncol):
                    if ast_used_dve:
                        sync.wait_ge(hDVE, ast_used_dve)
                    ldw_t[0] = qL.dma(sync, out=ast[0:2, 0:ncol],
                                      in_=hsrc[0:2, e0 * 128:e0 * 128 + ncol])
                block.sync(_)
                lda = ldw_t[0]

                def _(v, lda=lda, ncol=ncol):
                    qL.w(v, lda)
                    if astb_used_pe:
                        v.wait_ge(hPE, astb_used_pe)
                    dve_i(v.tensor_copy(out=astb[0:2, 0:ncol], in_=ast[0:2, 0:ncol]))
                block.vector(_)
                ast_used_dve = DVE[0]
                cast_c = DVE[0]

                for e in range(e0, e1):
                    abuf = a_eidx[0] % 2
                    a_eidx[0] += 1
                    trp2 = (ps[:, 3072:3073] if abuf == 0 else ps[:, 3584:3585]).bitcast(BF16)

                    def _(t, cast_c=cast_c, trp2=trp2, e=e, e0=e0, abuf=abuf):
                        t.wait_ge(hDVE, cast_c)
                        t.wait_ge(hACT, vevac_all)
                        if aT_ev[abuf]:
                            t.wait_ge(hDVE, aT_ev[abuf])
                        pe_i(t.transpose(out=trp2[:, :],
                                         in_=astb[0:2, (e - e0) * 128:(e - e0) * 128 + 128],
                                         identity=eyeb[0:2, 0:2]))
                    block.tensor(_)
                    trc = PE[0]

                    def _(v, dstT=dstT, e=e, trc=trc, trp2=trp2):
                        v.wait_ge(hPE, trc)
                        dve_i(v.tensor_copy(out=dstT[:, 2 * e:2 * e + 2], in_=trp2[:, :]))
                    block.vector(_)
                    aT_ev[abuf] = DVE[0]
                astb_used_pe = PE[0]

        u_evac_prev = [0]
        for (aT, udst) in [(a1T, u1b), (a3T, u3b)]:
            def _(t, aT=aT):
                t.wait_ge(hDVE, max(aT_ev))
                if u_evac_prev[0]:
                    t.wait_ge(hDVE, u_evac_prev[0])
                last = None
                for e in range(32):
                    for half in range(2):
                        last = t.matmul(out=ps[0:2, 2048 + half * 512:2048 + half * 512 + 512],
                                        lhsT=aT[:, 2 * e:2 * e + 2],
                                        rhs=xnT[:, e * TOK + half * 512:e * TOK + half * 512 + 512],
                                        start=(e == 0), stop=(e == 31))
                pe_i(last)
            block.tensor(_)
            umm = PE[0]

            def _(v, udst=udst, umm=umm):
                v.wait_ge(hPE, umm)
                dve_i(v.tensor_scalar(out=udst[0:2, :], in0=ps[0:2, 2048:3072],
                                      scalar1=con[0:2, 5:6], scalar2=None, op0=AL.mult))
            block.vector(_)
            u_evac_prev[0] = DVE[0]
        u_dve = DVE[0]
        kvctx.close()

        wqh = [An(nc.sbuf_tensor(f"wqh{i}", [128, 2048], F32)) for i in range(2)]
        wqb = [An(nc.sbuf_tensor(f"wqb{i}", [128, E], BF16)) for i in range(2)]
        A3_PE, A3_DVE = PE[0], DVE[0]
        qcast = [0, 0]
        qmm = [0, 0]
        qrope = [0, 0]
        wqh_cast = [0, 0]

        def emit_q_load(h):
            for half in range(2):
                def _(sync, h=h, half=half):
                    if h == 0 and half == 0:
                        qG.wsnap(sync, qG.snap())
                        sync.wait_ge(hPE, A3_PE)
                        sync.wait_ge(hDVE, A3_DVE)
                    if wqh_cast[half]:
                        sync.wait_ge(hDVE, wqh_cast[half])
                    ldw_t[0] = qL.dma(sync,
                        out=wqh[half][:, :].rearrange("p (e d) -> p e d", d=128),
                        in_=hwq.rearrange("(e p) d -> p e d", p=128)
                        [:, half * 16:half * 16 + 16, h * 128:h * 128 + 128])
                block.sync(_)
                ldq = ldw_t[0]

                def _(v, h=h, half=half, ldq=ldq):
                    qL.w(v, ldq)
                    if qmm[h % 2]:
                        v.wait_ge(hPE, qmm[h % 2])
                    if h == 0 and half == 0:
                        v.wait_ge(hDVE, u_dve)
                    wqh_cast[half] = dve_i(v.tensor_copy(
                        out=wqb[h % 2][:, half * 2048:half * 2048 + 2048], in_=wqh[half][:, :]))
                block.vector(_)
            qcast[h % 2] = DVE[0]

        emit_q_load(0)
        for h in range(NH):
            if h + 1 < NH:
                emit_q_load(h + 1)

            pbase = (h % 2) * 1024

            def _(t, h=h, pbase=pbase):
                t.wait_ge(hDVE, qcast[h % 2])
                if qrope[h % 2]:
                    t.wait_ge(hDVE, qrope[h % 2])
                if h == 0:
                    t.wait_ge(hDVE, u_dve)
                last = None
                for e in range(32):
                    for half in range(2):
                        last = t.matmul(
                            out=ps[:, pbase + half * 512:pbase + half * 512 + 512],
                            lhsT=wqb[h % 2][:, e * 128:e * 128 + 128],
                            rhs=xnT[:, e * TOK + half * 512:e * TOK + half * 512 + 512],
                            start=(e == 0), stop=(e == 31))
                qmm[h % 2] = pe_i(last)
            block.tensor(_)

            def _(v, h=h, pbase=pbase):
                v.wait_ge(hPE, qmm[h % 2])
                if rope_prev[0]:
                    v.wait_ge(hDVE, rope_prev[0])
                p = ps[:, pbase:pbase + 1024]
                bl = bqpp[0:64, h:h + 1]
                bh = bqpp[64:128, h:h + 1]
                dst = qT[:, h * TOK:(h + 1) * TOK]
                c1 = dve_i(v.scalar_tensor_tensor(out=rtmp[0:64, :], in0=p[64:128, :],
                                                  scalar=bh, in1=sinT[64:128, :],
                                                  op0=AL.add, op1=AL.mult))
                c2 = dve_i(v.scalar_tensor_tensor(out=rtmp[64:128, :], in0=p[0:64, :],
                                                  scalar=bl, in1=sinT[0:64, :],
                                                  op0=AL.add, op1=AL.mult))
                c3 = dve_i(v.scalar_tensor_tensor(out=rtmp2[0:64, :], in0=p[0:64, :],
                                                  scalar=bl, in1=cosT[0:64, :],
                                                  op0=AL.add, op1=AL.mult))
                c4 = dve_i(v.scalar_tensor_tensor(out=rtmp2[64:128, :], in0=p[64:128, :],
                                                  scalar=bh, in1=cosT[64:128, :],
                                                  op0=AL.add, op1=AL.mult))
                v.wait_ge(hDVE, c3)
                dve_i(v.tensor_sub(out=dst[0:64, :], in0=rtmp2[0:64, :], in1=rtmp[0:64, :]))
                v.wait_ge(hDVE, c4)
                dve_i(v.tensor_add(out=dst[64:128, :], in0=rtmp2[64:128, :], in1=rtmp[64:128, :]))
                qrope[h % 2] = DVE[0]
                rope_prev[0] = DVE[0]
            block.vector(_)
        qT_dve = DVE[0]
        qproj_pe = PE[0]

        def _(v):
            v.wait_ge(hDVE, qT_dve)
            for ep in range(8):
                if ep > 0:
                    v.wait_ge(hDVE, DVE[0])
                c1 = dve_i(v.tensor_add(out=rtmp[:, :].bitcast(BF16)[:, 0:1024],
                                        in0=xnT[:, ep * TOK:(ep + 1) * TOK],
                                        in1=xnT[:, (ep + 8) * TOK:(ep + 9) * TOK]))
                c2 = dve_i(v.tensor_add(out=rtmp2[:, :].bitcast(BF16)[:, 0:1024],
                                        in0=xnT[:, (ep + 16) * TOK:(ep + 17) * TOK],
                                        in1=xnT[:, (ep + 24) * TOK:(ep + 25) * TOK]))
                v.wait_ge(hDVE, c2)
                dve_i(v.tensor_add(out=xf[:, ep * TOK:(ep + 1) * TOK],
                                   in0=rtmp[:, :].bitcast(BF16)[:, 0:1024],
                                   in1=rtmp2[:, :].bitcast(BF16)[:, 0:1024]))
        block.vector(_)
        xf_dve = DVE[0]

        def _(gp):
            if "xnT" in TAPS:
                gp.wait_ge(hDVE, xnT_dve)
                gp.wait_ge(hACT, xnT_act)
                qG.dma(gp, out=taps["xnT"], in_=xnT[:, :])
            if "qT" in TAPS:
                gp.wait_ge(hDVE, qT_dve)
                qG.dma(gp, out=taps["qT"], in_=qT[:, :])
            if "cc" in TAPS:
                gp.wait_ge(hCC, 1)
                qG.dma(gp, out=taps["cc"], in_=cc_out.ap()[:, :])
            if "u1" in TAPS:
                gp.wait_ge(hDVE, u_dve)
                qG.dma(gp, out=taps["u1"], in_=u1b[0:2, :])
            if "u3" in TAPS:
                gp.wait_ge(hDVE, u_dve)
                qG.dma(gp, out=taps["u3"], in_=u3b[0:2, :])
            if "xf" in TAPS:
                gp.wait_ge(hDVE, xf_dve)
                qG.dma(gp, out=taps["xf"], in_=xf[:, :])
        block.gpsimd(_)

        actx.close()

        bctx = ExitStack()
        Bn = bctx.enter_context
        kTb = [Bn(nc.sbuf_tensor(f"kT{i}", [128, 4096], BF16)) for i in range(2)]
        vab = [Bn(nc.sbuf_tensor(f"va{i}", [128, 32 * 129], BF16)) for i in range(2)]
        pt = Bn(nc.sbuf_tensor("pt", [128, 4 * 1024], BF16))
        s_sb = r1

        def _(v):
            v.wait_ge(hDVE, xf_dve)
            dve_i(v.memset(vab[0][:, :], 1.0))
            dve_i(v.memset(vab[1][:, :], 1.0))
        block.vector(_)
        ones_dve = DVE[0]

        kv_tok = [[] for _ in range(8)]

        def _(gp):
            for g in range(8):
                if g == 0:
                    gp.wait_ge(hCC, 1)
                    gp.wait_ge(hDVE, ones_dve)
                    gp.wait_ge(hPE, qproj_pe)
                if g >= 2:
                    gp.wait_ge(hSC, (g - 1) * 128)
                    gp.wait_ge(hPV, (g - 1) * 128)
                kv_tok[g].append(qG.dma(gp,
                    out=kTb[g % 2][:, :].rearrange("p (r t) -> p r t", r=4),
                    in_=cc_out.ap().rearrange("(r q p) t -> p r q t", p=128, q=16)[:, :, g, :]))
                for r in range(4):
                    kv_tok[g].append(qG.dma(gp,
                        out=vab[g % 2][:, :].rearrange("p (t c) -> p t c", c=129)
                            [:, r * 8:r * 8 + 8, 0:128],
                        in_=cc_out.ap()[r * 2048 + 1024:r * 2048 + 2048,
                                        g * 128:g * 128 + 128]
                            .rearrange("(k p) d -> p k d", p=128)))
        block.gpsimd(_)

        NJ = 64 * 16
        DPH = 9
        sc_n = [0]
        pv_n = [0]

        def b_scores(t, J):
            hh, j = J // 16, J % 16
            g = hh // 8
            h = g * 4 + (hh // 2) % 4
            half = hh % 2
            if j == 0:
                for tok in kv_tok[g]:
                    qG.w(t, tok)
            if J >= 2:
                t.wait_ge(hEXP, J - 1)
            for e2 in range(2):
                kt = 2 * j + e2
                ins = t.matmul(
                    out=ps[:, (J % 2) * 1024 + e2 * 512:(J % 2) * 1024 + e2 * 512 + 512],
                    lhsT=kTb[g % 2][:, kt * 128:kt * 128 + 128],
                    rhs=qT[:, h * TOK + half * 512:h * TOK + half * 512 + 512],
                    start=True, stop=True)
                if e2 == 1:
                    sc_n[0] += 1
                    ins.then_inc(hSC, 1)

        def b_pv(t, J):
            hh, j = J // 16, J % 16
            g = hh // 8
            pb = 2048 + (hh % 2) * 1024
            t.wait_ge(hEXP, J + 1)
            if j == 0 and hh >= 2:
                t.wait_ge(hEV, (hh - 1) * DPH)
            for e2 in range(2):
                kt = 2 * j + e2
                for qt in range(4):
                    off = pb + (qt // 2) * 512 + (qt % 2) * 129
                    ins = t.matmul(
                        out=ps[:, off:off + 129],
                        lhsT=pt[:, (J % 4) * 1024 + e2 * 512 + qt * 128:
                                (J % 4) * 1024 + e2 * 512 + qt * 128 + 128],
                        rhs=vab[g % 2][:, kt * 129:kt * 129 + 129],
                        start=(j == 0 and e2 == 0 and qt % 2 == 0),
                        stop=(j == 15 and e2 == 1 and qt % 2 == 1))
                    if e2 == 1 and qt == 3:
                        pv_n[0] += 1
                        ins.then_inc(hPV, 1)

        def _(t):
            b_scores(t, 0)
            b_scores(t, 1)
            for J in range(NJ):
                if J + 2 < NJ:
                    b_scores(t, J + 2)
                b_pv(t, J)
        block.tensor(_)

        def _(a):
            qL.wsnap(a, ld_const)
            for J in range(NJ):
                a.wait_ge(hSC, J + 1)
                if J >= 4:
                    a.wait_ge(hPV, J - 3)
                a.activation(out=pt[:, (J % 4) * 1024:(J % 4) * 1024 + 1024],
                             in_=ps[:, (J % 2) * 1024:(J % 2) * 1024 + 1024],
                             func=AF.Exp, bias=con[:, 0:1], scale=con[:, 1:2]).then_inc(hEXP, 1)
        block.scalar(_)

        ev_n = [0]

        def _(v):
            v.wait_ge(hACT, xnT_act)
            v.wait_ge(hDVE, xf_dve)
            for hh in range(64):
                g = hh // 8
                h = g * 4 + (hh // 2) % 4
                half = hh % 2
                pb = 2048 + (hh % 2) * 1024
                base = (hh % 2) * 4
                v.wait_ge(hPV, (hh + 1) * 16)
                if hh >= 1:
                    v.wait_ge(hEV, ev_n[0])
                for qt in range(4):
                    off = pb + (qt // 2) * 512 + (qt % 2) * 129 + 128
                    v.tensor_copy(out=lsb[:, qt:qt + 1], in_=ps[:, off:off + 1]).then_inc(hEV, 1)
                    ev_n[0] += 1
                v.wait_ge(hEV, ev_n[0])
                v.reciprocal(out=recb[:, base:base + 4], in_=lsb[:, 0:4]).then_inc(hEV, 1)
                ev_n[0] += 1
                rcp = ev_n[0]
                for qt in range(4):
                    tq = half * 4 + qt
                    v.wait_ge(hEV, rcp)
                    v.tensor_scalar(
                        out=s_sb[:, tq * E + h * 128:tq * E + h * 128 + 128],
                        in0=ps[:, pb + (qt // 2) * 512 + (qt % 2) * 129:
                               pb + (qt // 2) * 512 + (qt % 2) * 129 + 128],
                        scalar1=recb[:, base + qt:base + qt + 1], scalar2=None,
                        op0=AL.mult).then_inc(hEV, 1)
                    ev_n[0] += 1
        block.vector(_)
        B_EV = 64 * DPH

        def _(gp):
            if "s" in TAPS:
                gp.wait_ge(hEV, B_EV)
                qG.dma(gp, out=taps["s"], in_=s_sb[:, :])
        block.gpsimd(_)

        bctx.close()

        cctx = ExitStack()
        Cn = cctx.enter_context
        yTov = Cn(nc.sbuf_tensor("yTov", [128, (NHT - 64) * 512], BF16))
        w1f = [Cn(nc.sbuf_tensor(f"w1f{i}", [128, 8 * 128], F32)) for i in range(2)]
        w3f = [Cn(nc.sbuf_tensor(f"w3f{i}", [128, 8 * 128], F32)) for i in range(2)]
        w1bb = [Cn(nc.sbuf_tensor(f"w1b{i}", [128, 8 * 128], BF16)) for i in range(2)]
        w3bb = [Cn(nc.sbuf_tensor(f"w3b{i}", [128, 8 * 128], BF16)) for i in range(2)]
        bsl = Cn(nc.sbuf_tensor("bsl", [2, 512], F32))
        bs_all = Cn(nc.sbuf_tensor("bs_all", [2, 1024], BF16))
        bslb = [bs_all[0:2, 0:256], bs_all[0:2, 256:512]]
        ysig = [rtmp2[:, 0:512], rtmp2[:, 512:1024]]
        w2fp = [rtmp[:, 0:1024], w3f[0]]
        b2sl = bsl
        b2slb = bs_all[0:2, 512:1024]
        xrl = [w1f[0][:, 0:512], w1f[0][:, 512:1024]]
        ost = [w1f[1][:, 0:512], w1f[1][:, 512:1024]]
        yta = rtmp[:, 0:256].bitcast(BF16)
        ytb = rtmp[:, 256:512].bitcast(BF16)

        def yt(ht):
            if ht < 64:
                return r2[:, ht * 512:ht * 512 + 512]
            return yTov[:, (ht - 64) * 512:(ht - 64) * 512 + 512]

        lin2_pe_done = 0
        g_tap_tok = [None]
        prev_ymul_end = [0]
        last_lin2_act = [0]
        lin2_gd_end = [[]]
        lin2_dve_end = [0]
        for tc in range(2):
            wcast1 = [0, 0]
            wcast3 = [0, 0]
            wused = [0, 0]
            bcast = [0, 0]
            bused = [0, 0]
            ymul_l = [0] * NHT
            mm13_l = [0] * NHT
            bcast_last = [0]
            ysig_war = [0]
            yT_dve = [0] * NHT
            actC0 = ACT[0]
            silu1 = lambda ht: actC0 + ht + 1

            def emit_w13_load(tc, ht):
                def _(sync, tc=tc, ht=ht):
                    if tc == 0 and ht == 0:
                        sync.wait_ge(hSC, NJ)
                        sync.wait_ge(hPV, NJ)
                        sync.wait_ge(hEXP, NJ)
                        sync.wait_ge(hEV, B_EV)
                    if tc == 1 and ht <= 1:
                        qG.wsnap(sync, lin2_gd_end[0])
                        sync.wait_ge(hDVE, lin2_dve_end[0])
                    if wcast1[ht % 2]:
                        sync.wait_ge(hDVE, wcast1[ht % 2])
                    t1 = qL.dma(sync,
                        out=w1f[ht % 2][:, :].rearrange("p (k h) -> p k h", h=128),
                        in_=hw1.rearrange("(k p) h -> p k h", p=128)[:, :, ht * 128:ht * 128 + 128])
                    if wcast3[ht % 2]:
                        sync.wait_ge(hDVE, wcast3[ht % 2])
                    t2 = qL.dma(sync,
                        out=w3f[ht % 2][:, :].rearrange("p (k h) -> p k h", h=128),
                        in_=hw3.rearrange("(k p) h -> p k h", p=128)[:, :, ht * 128:ht * 128 + 128])
                    if bcast_last[0]:
                        sync.wait_ge(hDVE, bcast_last[0])
                    t3 = qL.dma(sync, out=bsl[0:2, 0:128],
                                in_=hb1[0:2, ht * 128:ht * 128 + 128])
                    t4 = qL.dma(sync, out=bsl[0:2, 128:256],
                                in_=hb3[0:2, ht * 128:ht * 128 + 128])
                    ldw_t[0] = (t1, t2, t3, t4)
                block.sync(_)
                lds = ldw_t[0]

                def _(v, tc=tc, ht=ht, lds=lds):
                    for tk in lds:
                        qL.w(v, tk)
                    if tc == 1 and ht <= 1 and lin2_pe_done:
                        v.wait_ge(hPE, lin2_pe_done)
                    if wused[ht % 2]:
                        v.wait_ge(hPE, wused[ht % 2])
                    dve_i(v.tensor_copy(out=w1bb[ht % 2][:, :], in_=w1f[ht % 2][:, :]))
                    wcast1[ht % 2] = DVE[0]
                    dve_i(v.tensor_copy(out=w3bb[ht % 2][:, :], in_=w3f[ht % 2][:, :]))
                    wcast3[ht % 2] = DVE[0]
                    if bused[ht % 2]:
                        v.wait_ge(hPE, bused[ht % 2])
                    dve_i(v.tensor_copy(out=bslb[ht % 2][0:2, :], in_=bsl[0:2, 0:256]))
                    bcast[ht % 2] = DVE[0]
                    bcast_last[0] = DVE[0]
                block.vector(_)

            emit_w13_load(tc, 0)
            for ht in range(NHT):
                cur_cast = max(wcast1[ht % 2], wcast3[ht % 2], bcast[ht % 2])
                if ht + 1 < NHT:
                    emit_w13_load(tc, ht + 1)

                p1 = ps[:, (ht % 2) * 512:(ht % 2) * 512 + 512]
                p3 = ps[:, 1024 + (ht % 2) * 512:1024 + (ht % 2) * 512 + 512]

                def _(t, tc=tc, ht=ht, cur_cast=cur_cast, p1=p1, p3=p3):
                    t.wait_ge(hDVE, cur_cast)
                    if tc == 0 and ht == 0:
                        t.wait_ge(hEV, B_EV)
                        t.wait_ge(hEXP, NJ)
                        t.wait_ge(hDVE, xf_dve)
                    if tc == 1 and ht == 0:
                        t.wait_ge(hACT, last_lin2_act[0])
                    if ht >= 2:
                        t.wait_ge(hACT, silu1(ht - 2))
                        t.wait_ge(hDVE, ymul_l[ht - 2])
                    for k in range(8):
                        t.matmul(out=p1[:, :], lhsT=w1bb[ht % 2][:, k * 128:k * 128 + 128],
                                 rhs=xf[:, k * TOK + tc * 512:k * TOK + tc * 512 + 512],
                                 start=(k == 0), stop=False)
                    t.matmul(out=p1[:, :], lhsT=bslb[ht % 2][0:2, 0:128],
                             rhs=u1b[0:2, tc * 512:tc * 512 + 512], start=False, stop=True)
                    for k in range(8):
                        t.matmul(out=p3[:, :], lhsT=w3bb[ht % 2][:, k * 128:k * 128 + 128],
                                 rhs=xf[:, k * TOK + tc * 512:k * TOK + tc * 512 + 512],
                                 start=(k == 0), stop=False)
                    last = t.matmul(out=p3[:, :], lhsT=bslb[ht % 2][0:2, 128:256],
                                    rhs=u3b[0:2, tc * 512:tc * 512 + 512], start=False, stop=True)
                    pe_i(last)
                block.tensor(_)
                mm13_l[ht] = PE[0]
                wused[ht % 2] = PE[0]
                bused[ht % 2] = PE[0]

                def _(v, tc=tc, ht=ht, p3=p3):
                    v.wait_ge(hACT, silu1(ht))
                    if tc == 1 and ht < NYF and lin2_pe_done:
                        v.wait_ge(hPE, lin2_pe_done)
                    dve_i(v.tensor_mul(out=yt(ht), in0=ysig[ht % 2][:, :], in1=p3[:, :]))
                block.vector(_)
                ymul_l[ht] = DVE[0]
                ysig_war[0] = DVE[0]
                yT_dve[ht] = DVE[0]

                if ht >= 1:
                    hp = ht - 1

                    def _(t, hp=hp):
                        t.wait_ge(hDVE, yT_dve[hp])
                        t.matmul(out=ps[0:2, 2048:2560], lhsT=a2T[:, 2 * hp:2 * hp + 2],
                                 rhs=yt(hp)[0:128, :], start=(hp == 0), stop=False)
                    block.tensor(_)

            def _(a, tc=tc, mm13_l=mm13_l, ymul_l=ymul_l, actC0=actC0):
                for ht in range(NHT):
                    a.wait_ge(hPE, mm13_l[ht])
                    if ht >= 2:
                        a.wait_ge(hDVE, ymul_l[ht - 2])
                    elif tc == 1 and ht < 2 and prev_ymul_end[0]:
                        a.wait_ge(hDVE, prev_ymul_end[0])
                    p1 = ps[:, (ht % 2) * 512:(ht % 2) * 512 + 512]
                    act_i(a.activation(out=ysig[ht % 2][:, :], in_=p1[:, :], func=AF.Silu))
            block.scalar(_)
            assert ACT[0] == silu1(NHT - 1)

            def _(t):
                t.wait_ge(hDVE, yT_dve[NHT - 1])
                pe_i(t.matmul(out=ps[0:2, 2048:2560], lhsT=a2T[:, 2 * (NHT - 1):2 * (NHT - 1) + 2],
                              rhs=yt(NHT - 1)[0:128, :], start=False, stop=True))
            block.tensor(_)
            u2mm = PE[0]

            def _(v, u2mm=u2mm):
                v.wait_ge(hPE, u2mm)
                if lin2_pe_done:
                    v.wait_ge(hPE, lin2_pe_done)
                dve_i(v.tensor_scalar(out=u2b[0:2, :], in0=ps[0:2, 2048:2560],
                                      scalar1=con[0:2, 5:6], scalar2=None, op0=AL.mult))
            block.vector(_)
            u2_dve = DVE[0]

            def _(v, yT_dve=yT_dve):
                v.wait_ge(hDVE, yT_dve[NHT - 1])
                for f in range(NYF):
                    if f > 0:
                        v.wait_ge(hDVE, DVE[0])
                    if f < 21:
                        c1 = dve_i(v.tensor_add(out=yta[:, :], in0=yt(f), in1=yt(f + 43)))
                        c2 = dve_i(v.tensor_add(out=ytb[0:64, :],
                                                in0=yt(f + 21)[64:128, :],
                                                in1=yt(f + 64)[64:128, :]))
                        c3 = dve_i(v.tensor_add(out=ytb[64:128, :],
                                                in0=yt(f + 22)[0:64, :],
                                                in1=yt(f + 65)[0:64, :]))
                        v.wait_ge(hDVE, c3)
                        dve_i(v.tensor_add(out=yt(f), in0=yta[:, :], in1=ytb[:, :]))
                    else:
                        c1 = dve_i(v.tensor_add(out=yta[0:64, :], in0=yt(21)[0:64, :],
                                                in1=yt(64)[0:64, :]))
                        c2 = dve_i(v.tensor_add(out=ytb[0:64, :], in0=yt(42)[64:128, :],
                                                in1=yt(85)[64:128, :]))
                        v.wait_ge(hDVE, c2)
                        dve_i(v.tensor_add(out=yt(21)[0:64, :], in0=yta[0:64, :],
                                           in1=ytb[0:64, :]))
            block.vector(_)
            yf_dve = DVE[0]

            if "yT" in TAPS and tc == 0:
                def _(gp):
                    gp.wait_ge(hDVE, yf_dve)
                    qG.dma(gp, out=taps["yT"], in_=r2[:, 0:4 * 512])
                block.gpsimd(_)

            w2cast = [0, 0]
            w2used = [0, 0]
            wbuf_pe = [0, 0]
            peb = [0]
            b2cast = 0
            b2used = 0
            xld = [None, None]
            xuse = [0, 0]
            ouse = [None, None]
            ost_dma = [None, None]
            mmlin2_l = [0] * 8
            mult2_l = [0] * 32
            ysig_war0 = ysig_war[0]
            actC1 = ACT[0]
            silu2 = lambda i: actC1 + i + 1

            def emit_combine(tc, ec):
                for tt in range(4):
                    tq = tc * 4 + tt
                    i2 = ec * 4 + tt
                    buf = i2 % 2

                    def _(sync, tc=tc, ec=ec, tt=tt, tq=tq, buf=buf):
                        if ec == 0 and tt == 0:
                            sync.wait_ge(hDVE, wcast1[0])
                            sync.wait_ge(hDVE, wcast1[1])
                        if xuse[buf]:
                            sync.wait_ge(hDVE, xuse[buf])
                        xld[buf] = qL.dma(sync,
                            out=xrl[buf][:, :],
                            in_=hx[tq * 128:tq * 128 + 128, ec * 512:ec * 512 + 512])
                    block.sync(_)

                    def _(v, tc=tc, ec=ec, tt=tt, tq=tq, buf=buf, i2=i2):
                        v.wait_ge(hACT, silu2(i2))
                        if ouse[buf]:
                            qG.w(v, ouse[buf])
                        c1 = dve_i(v.tensor_mul(out=ost[buf][:, :], in0=ysig[buf][:, :],
                                                in1=s_sb[:, tq * E + ec * 512:tq * E + ec * 512 + 512]))
                        mult2_l[i2] = c1
                        qL.w(v, xld[buf])
                        v.wait_ge(hDVE, c1)
                        dve_i(v.tensor_add(out=ost[buf][:, :], in0=ost[buf][:, :],
                                           in1=xrl[buf][:, :]))
                        xuse[buf] = DVE[0]
                    block.vector(_)
                    addc = DVE[0]

                    def _(gp, tq=tq, ec=ec, buf=buf, addc=addc):
                        gp.wait_ge(hDVE, addc)
                        ost_dma[buf] = qG.dma(gp,
                            out=hout[tq * 128:tq * 128 + 128, ec * 512:ec * 512 + 512],
                            in_=ost[buf][:, :])
                        ouse[buf] = ost_dma[buf]
                    block.gpsimd(_)

            for ec in range(8):
                def _(sync, tc=tc, ec=ec):
                    if ec == 0:
                        sync.wait_ge(hDVE, bcast[0])
                        sync.wait_ge(hDVE, bcast[1])
                    if b2cast:
                        sync.wait_ge(hDVE, b2cast)
                    ldw_t[0] = qL.dma(sync, out=b2sl[0:2, 0:512],
                                      in_=hb2[0:2, ec * 512:ec * 512 + 512])
                block.sync(_)
                ldb2 = ldw_t[0]

                def _(v, ldb2=ldb2):
                    qL.w(v, ldb2)
                    if b2used:
                        v.wait_ge(hPE, b2used)
                    dve_i(v.tensor_copy(out=b2slb[0:2, :], in_=b2sl[0:2, 0:512]))
                block.vector(_)
                b2cast = DVE[0]

                cast_c = [0] * 11
                ld_c = [None] * 11

                def emit_w2pair(tc, ec, j):
                    def _(sync, tc=tc, ec=ec, j=j):
                        if ec == 0 and j == 0:
                            sync.wait_ge(hDVE, yf_dve)
                        if ec == 0 and j == 1:
                            sync.wait_ge(hDVE, wcast3[0])
                            sync.wait_ge(hDVE, wcast3[1])
                        if w2cast[j % 2]:
                            sync.wait_ge(hDVE, w2cast[j % 2])
                        if j < 10:
                            ldw_t[0] = (qL.dma(sync,
                                out=w2fp[j % 2][:, :].rearrange("p (a c) -> p a c", c=512),
                                in_=hw2[j * 256:j * 256 + 256, ec * 512:ec * 512 + 512]
                                    .rearrange("(a p) c -> p a c", p=128)),)
                        else:
                            t1 = qL.dma(sync, out=w2fp[j % 2][:, 0:512],
                                        in_=hw2[2560:2688, ec * 512:ec * 512 + 512])
                            t2 = qL.dma(sync, out=w2fp[j % 2][0:64, 512:1024],
                                        in_=hw2[2688:2752, ec * 512:ec * 512 + 512])
                            ldw_t[0] = (t1, t2)
                    block.sync(_)
                    ld_c[j] = ldw_t[0]

                    def _(v, j=j):
                        for tk in ld_c[j]:
                            qL.w(v, tk)
                        if wbuf_pe[j % 2]:
                            v.wait_ge(hPE, wbuf_pe[j % 2])
                        else:
                            v.wait_ge(hPE, mm13_l[NHT - 1])
                        dve_i(v.tensor_copy(out=w1bb[j % 2][:, :], in_=w2fp[j % 2][:, :]))
                        cast_c[j] = DVE[0]
                    block.vector(_)
                    w2cast[j % 2] = DVE[0]

                if ec > 0:
                    wbuf_pe[0] = PE[0]
                    wbuf_pe[1] = PE[0] - 1
                peb[0] = PE[0]
                emit_w2pair(tc, ec, 0)
                emit_w2pair(tc, ec, 1)
                if ec >= 1:
                    emit_combine(tc, ec - 1)

                for j in range(11):
                    if j + 2 < 11:
                        wbuf_pe[j % 2] = peb[0] + j + 1
                        emit_w2pair(tc, ec, j + 2)
                    castc = cast_c[j]

                    def _(t, tc=tc, ec=ec, j=j, castc=castc):
                        t.wait_ge(hDVE, castc)
                        if j == 0:
                            t.wait_ge(hDVE, u2_dve)
                            if ec >= 1:
                                t.wait_ge(hACT, silu2((ec - 1) * 4 + 3))
                        if j == 10:
                            t.wait_ge(hDVE, b2cast)
                        last = None
                        for k in (2 * j, 2 * j + 1):
                            kr = 128 if k < 21 else 64
                            for tt in range(4):
                                last = t.matmul(
                                    out=ps[:, tt * 512:tt * 512 + 512],
                                    lhsT=yt(k)[0:kr, tt * 128:tt * 128 + 128],
                                    rhs=w1bb[j % 2][0:kr, (k % 2) * 512:(k % 2) * 512 + 512],
                                    start=(k == 0), stop=False)
                        if j == 10:
                            for tt in range(4):
                                last = t.matmul(
                                    out=ps[:, tt * 512:tt * 512 + 512],
                                    lhsT=u2b[0:2, tt * 128:tt * 128 + 128],
                                    rhs=b2slb[0:2, :], start=False, stop=True)
                        pe_i(last)
                    block.tensor(_)
                    w2used[j % 2] = PE[0]
                    if j == 10:
                        b2used = PE[0]
                mmlin2_l[ec] = PE[0]

            emit_combine(tc, 7)

            def _(a, tc=tc, mmlin2_l=mmlin2_l, mult2_l=mult2_l, ysig_war0=ysig_war0):
                for ec in range(8):
                    a.wait_ge(hPE, mmlin2_l[ec])
                    for tt in range(4):
                        i2 = ec * 4 + tt
                        if i2 >= 2:
                            a.wait_ge(hDVE, mult2_l[i2 - 2])
                        else:
                            a.wait_ge(hDVE, ysig_war0)
                        act_i(a.activation(out=ysig[i2 % 2][:, :],
                                           in_=ps[:, tt * 512:tt * 512 + 512], func=AF.Silu))
            block.scalar(_)
            assert ACT[0] == silu2(31)
            prev_ymul_end[0] = DVE[0]
            lin2_pe_done = PE[0]
            last_lin2_act[0] = ACT[0]
            lin2_gd_end[0] = qG.snap()
            lin2_dve_end[0] = DVE[0]

        def _(gp):
            qG.wsnap(gp, qG.snap())
            qL.wsnap(gp, qL.snap())
        block.gpsimd(_)

        cctx.close()
    return nc


def _host_inputs(inputs, core):
    b, c = core // 4, core % 4
    x = np.ascontiguousarray(np.asarray(inputs["L0_attn_x"])[b, c * 1024:(c + 1) * 1024, :])
    d = {"x": x}
    d["wq"] = _W["wq"]
    d["wk"] = _W["wk"]
    d["wv"] = _W["wv"]
    d["bq2"] = _W["bq2"]
    d["bk2"] = _W["bk2"]
    d["bv"] = _W["bv"]
    d["w1qT"] = _W["w1qT"]
    d["w3qT"] = _W["w3qT"]
    d["w2qT"] = _W["w2qT"]
    d["a1"] = _W["a1"]
    d["a3"] = _W["a3"]
    d["a2"] = _W["a2"]
    d["b1T"] = _W["b1T"]
    d["b3T"] = _W["b3T"]
    d["b2T"] = _W["b2T"]
    d["cosh"] = _ROPE[0][c]
    d["sinh"] = _ROPE[1][c]
    d["eyebh"] = np.eye(128, dtype=ml_dtypes.bfloat16)
    d["eyefh"] = np.eye(32, dtype=np.float32)
    d["conh"] = _CON
    return d


_W = {}
_ROPE = None
_CON = np.tile(np.array([[SHIFT, SCALE, EPS, -0.5, 1.0 / 4096.0, 0.5, -1.0, 1.0]],
                        dtype=np.float32), (128, 1))


def kernel(**inputs):
    global _ROPE
    asnp = lambda a: np.ascontiguousarray(np.asarray(a, dtype=np.float32))
    _W["wq"] = asnp(inputs["L0_attn_wq"])
    _W["wk"] = asnp(inputs["L0_attn_wk"])
    _W["wv"] = asnp(inputs["L0_attn_wv"])
    _W["bq2"] = asnp(inputs["L0_attn_bq"]).reshape(32, 128)
    _W["bk2"] = asnp(inputs["L0_attn_bk"]).reshape(8, 128)
    _W["bv"] = asnp(inputs["L0_attn_bv"]).reshape(1, 1024)
    _W["w1qT"] = asnp(np.asarray(inputs["L0_ffn_w1q"]).T)
    _W["w3qT"] = asnp(np.asarray(inputs["L0_ffn_w3q"]).T)
    _W["w2qT"] = asnp(np.asarray(inputs["L0_ffn_w2q"]).T)
    _W["a1"] = asnp(inputs["L0_ffn_a1"])
    _W["a3"] = asnp(inputs["L0_ffn_a3"])
    _W["a2"] = asnp(inputs["L0_ffn_a2"])
    _W["b1T"] = asnp(np.asarray(inputs["L0_ffn_b1"]).T)
    _W["b3T"] = asnp(np.asarray(inputs["L0_ffn_b3"]).T)
    _W["b2T"] = asnp(np.asarray(inputs["L0_ffn_b2"]).T)
    inv = (1.0 / 10000.0 ** (np.arange(0, 128, 2, dtype=np.float32) / np.float32(128.0))
           ).astype(np.float32)
    pos = np.arange(4096, dtype=np.float32)
    ang = pos[:, None] * inv[None, :]
    cos = np.cos(ang).astype(np.float32)
    sin = np.sin(ang).astype(np.float32)
    cosC = [np.ascontiguousarray(np.tile(cos[c * 1024:(c + 1) * 1024].T, (2, 1)))
            for c in range(4)]
    sinC = [np.ascontiguousarray(np.tile(sin[c * 1024:(c + 1) * 1024].T, (2, 1)))
            for c in range(4)]
    _ROPE = (cosC, sinC)

    nc = build_nc()
    in_maps = [_host_inputs(inputs, core) for core in range(8)]
    res = run_bass_kernel_spmd(nc, in_maps, core_ids=list(range(8)))
    out = np.empty((2, 4096, 4096), dtype=np.float32)
    for core in range(8):
        b, c = core // 4, core % 4
        out[b, c * 1024:(c + 1) * 1024, :] = res.results[core]["out"]
    return out
